# version 16
# baseline (speedup 1.0000x reference)
import numpy as np
import ml_dtypes
import concourse.bass as bass
import concourse.mybir as mybir
from concourse.bass_utils import run_bass_kernel_spmd
from contextlib import ExitStack

F32 = mybir.dt.float32
BF16 = mybir.dt.bfloat16
AF = mybir.ActivationFunctionType
ALU = mybir.AluOpType
AX = mybir.AxisListType.X

NCORE = 8
T = 8192
TC = T // NCORE
D = 2048
H, DK, DV, M = 16, 64, 128, 64
C = 128
NCH = TC // C
HFF = 8192
NA = HFF // 128
EPS = 1e-5
E64 = float(np.exp(-64.0))
FWHT_SCALE = float(np.float32(1.0) / np.float32(np.sqrt(np.float32(8192.0))))

ENGS = ("sp", "gp", "se", "ve", "te")


class TBuf:
    __slots__ = ("w", "r", "multi", "name")

    def __init__(self, name=""):
        self.w = None
        self.r = []
        self.multi = []
        self.name = name


class Chan:
    __slots__ = ("name", "sem", "cnt")

    def __init__(self, name):
        self.name = name
        self.sem = None
        self.cnt = 0


class Op:
    __slots__ = ("eng", "fn", "deps", "chan", "cc", "inc", "gseq", "target", "ph")

    def __init__(self, eng, fn, deps, chan, gseq, cc=False):
        self.eng, self.fn, self.deps, self.chan, self.gseq = eng, fn, deps, chan, gseq
        self.cc = cc
        self.inc = False
        self.target = 0

    @property
    def dma(self):
        return self.chan is not None


class Prog:
    def __init__(self):
        self.ops = []
        self.by_eng = {e: [] for e in ENGS}
        self.chans = []
        self.phase = "init"

    def chan(self, name):
        c = Chan(name)
        self.chans.append(c)
        return c

    def op(self, eng, fn, reads=(), writes=(), appends=(), dma=None, cc=False):
        deps = set()
        for b in reads:
            if b.w is not None:
                deps.add(b.w)
            deps.update(b.multi)
        for b in writes:
            if b.w is not None:
                deps.add(b.w)
            for r in b.r:
                deps.add(r)
            deps.update(b.multi)
        for b in appends:
            if b.w is not None:
                deps.add(b.w)
        o = Op(eng, fn, deps, dma, len(self.ops), cc=cc)
        o.ph = self.phase
        self.ops.append(o)
        self.by_eng[eng].append(o)
        for b in reads:
            b.r.append(o)
        for b in writes:
            b.w = o
            b.r = []
            b.multi = []
        for b in appends:
            b.multi.append(o)
        return o

    @staticmethod
    def _implicit(d, o):
        return d.eng == o.eng == "te" and not d.dma and not o.dma

    def emit(self, nc, block, sems_c, sem_cc, mksem):
        for o in self.ops:
            for d in o.deps:
                assert d.gseq < o.gseq
                if self._implicit(d, o):
                    continue
                d.inc = True
        cnt_c = {e: 0 for e in ENGS}
        for e in ENGS:
            for o in self.by_eng[e]:
                if o.cc:
                    o.target = 1
                elif o.dma:
                    o.chan.cnt += 16
                    o.target = o.chan.cnt
                else:
                    if o.inc:
                        cnt_c[e] += 1
                    o.target = cnt_c[e]
        for c in self.chans:
            if c.cnt > 0:
                c.sem = mksem("ch_" + c.name)
        self.final_counts = cnt_c
        prog = self

        def emit_engine(eng_obj, e):
            waited = {}
            cur_ph = None
            sid = None
            for o in prog.by_eng[e]:
                if o.ph != cur_ph:
                    if sid is not None:
                        nc.leave_named_scope(f"{cur_ph}_{e}", sid, False)
                    cur_ph = o.ph
                    sid, _ = nc.enter_named_scope(f"{cur_ph}_{e}", False)
                for d in sorted(o.deps, key=lambda d: d.gseq):
                    if Prog._implicit(d, o):
                        continue
                    if d.cc:
                        sem, key = sem_cc, ("cc",)
                    elif d.dma:
                        sem, key = d.chan.sem, ("ch", id(d.chan))
                    else:
                        sem, key = sems_c[d.eng], (d.eng,)
                    if waited.get(key, -1) >= d.target:
                        continue
                    eng_obj.wait_ge(sem, d.target)
                    waited[key] = d.target
                inst = o.fn(eng_obj)
                if o.cc:
                    inst.then_inc(sem_cc)
                elif o.dma:
                    inst.then_inc(o.chan.sem, 16)
                elif o.inc:
                    assert inst is not None
                    inst.then_inc(sems_c[e], 1)
            if sid is not None:
                nc.leave_named_scope(f"{cur_ph}_{e}", sid, False)
            if e == "sp":
                for c in prog.chans:
                    if c.cnt > 0:
                        eng_obj.wait_ge(c.sem, c.cnt)

        @block.sync
        def _(sync):
            emit_engine(sync, "sp")

        @block.gpsimd
        def _(gp):
            emit_engine(gp, "gp")

        @block.scalar
        def _(se):
            emit_engine(se, "se")

        @block.vector
        def _(ve):
            emit_engine(ve, "ve")

        @block.tensor
        def _(te):
            emit_engine(te, "te")


def build_program(dump=(), phases=99):
    nc = bass.Bass("TRN2", target_bir_lowering=False)
    dump = set(dump)

    x_in = nc.dram_tensor("x", [TC, D], F32, kind="ExternalInput").ap()
    wqS = nc.dram_tensor("wqS", [8, 128, 2048], F32, kind="ExternalInput").ap()
    wkS = nc.dram_tensor("wkS", [8, 128, 2048], F32, kind="ExternalInput").ap()
    wsS = nc.dram_tensor("wsS", [8, 128, 2048], F32, kind="ExternalInput").ap()
    wvS = nc.dram_tensor("wvS", [128, 2, 16384], F32, kind="ExternalInput").ap()
    wgS = nc.dram_tensor("wgS", [128, 2, 16384], F32, kind="ExternalInput").ap()
    woS = nc.dram_tensor("woS", [128, 32768], F32, kind="ExternalInput").ap()
    w1S = nc.dram_tensor("w1S", [NA, 128, 2048], F32, kind="ExternalInput").ap()
    w3S = nc.dram_tensor("w3S", [NA, 128, 2048], F32, kind="ExternalInput").ap()
    w2S = nc.dram_tensor("w2S", [16, 128, 8192], F32, kind="ExternalInput").ap()
    gnw_in = nc.dram_tensor("gnw", [128, 1], F32, kind="ExternalInput").ap()
    tri_in = nc.dram_tensor("tri", [128, 128], F32, kind="ExternalInput").ap()
    idn_in = nc.dram_tensor("idn", [128, 128], F32, kind="ExternalInput").ap()
    h64d_in = nc.dram_tensor("h64d", [128, 128], F32, kind="ExternalInput").ap()
    h128_in = nc.dram_tensor("h128c", [128, 128], F32, kind="ExternalInput").ap()
    ones_in = nc.dram_tensor("onesc", [128, 128], F32, kind="ExternalInput").ap()
    pmask_in = nc.dram_tensor("pmask", [128, 8], F32, kind="ExternalInput").ap()
    out_dram = nc.dram_tensor("out", [TC, D], F32, kind="ExternalOutput").ap()

    cc_in = nc.dram_tensor("cc_in", [128, 2560], BF16, kind="Internal")
    cc_out = nc.dram_tensor("cc_out", [1024, 2560], BF16, kind="Internal")
    scr1 = nc.dram_tensor("scr1", [HFF, TC], BF16, kind="Internal").ap()
    scrX1 = nc.dram_tensor("scrX1", [TC, 2048], F32, kind="Internal").ap()
    woB = nc.dram_tensor("woB", [128, 32768], BF16, kind="Internal").ap()
    w2B = nc.dram_tensor("w2B", [16, 128, 8192], BF16, kind="Internal").ap()

    dump_aps = {}

    def dumpt(name, shape, dtype=F32):
        if name in dump:
            dump_aps[name] = nc.dram_tensor("dmp_" + name, shape, dtype,
                                            kind="ExternalOutput").ap()
            return dump_aps[name]
        return None

    P = Prog()
    tb_scr1 = TBuf("scr1d")
    tb_scrX1 = TBuf("scrX1d")
    ctx = ExitStack()
    E_ = ctx.enter_context

    cur_barrier = []

    def mkbuf(name=""):
        tb = TBuf(name)
        tb.r = list(cur_barrier)
        return tb

    def free_scope(scope, tbufs):
        ops = []
        for tb in tbufs:
            if tb.w is not None:
                ops.append(tb.w)
            ops.extend(tb.r)
        cur_barrier.extend(ops)
        scope.close()

    def sb(name, shape, dtype, scope=None, side=None):
        kw = {"side": side} if side else {}
        return (scope or ctx).enter_context(nc.sbuf_tensor(name, shape, dtype, **kw))

    psA = E_(nc.psum_tensor([128, 512], F32))
    psB = E_(nc.psum_tensor([128, 512], F32))
    psC = E_(nc.psum_tensor([128, 512], F32))
    psD = E_(nc.psum_tensor([128, 512], F32))
    psE = E_(nc.psum_tensor([128, 1024], F32))
    psF = E_(nc.psum_tensor([128, 1024], BF16))
    psG = E_(nc.psum_tensor([128, 1024], BF16))
    tA, tB, tC_, tD = TBuf("psA"), TBuf("psB"), TBuf("psC"), TBuf("psD")
    tE, tF, tG = TBuf("psE"), TBuf("psF"), TBuf("psG")

    sems_c = {e: E_(nc.semaphore(f"s_c_{e}")) for e in ENGS}
    sem_cc = E_(nc.semaphore("s_cc"))
    mksem = lambda n: E_(nc.semaphore(n))
    ch_const = P.chan("const")
    ch_x = [P.chan(f"x{i}") for i in range(2)]
    ch_w = [P.chan(f"w{i}") for i in range(3)]
    ch_wbig = P.chan("wbig")
    ch_dump = P.chan("dump")
    ch_cc = P.chan("ccin")
    ch_rk = [P.chan(f"rk{i}") for i in range(4)]
    ch_out = P.chan("outp")

    consts32 = sb("consts32", [128, 512], F32)
    h64d_32 = sb("h64d_32", [128, 128], F32)
    constsbf = sb("constsbf", [128, 512], BF16)
    h64d_bf = sb("h64d_bf", [128, 128], BF16)
    gnw_sb = sb("gnw_sb", [128, 1], F32)
    pmask_sb = sb("pmask_sb", [128, 8], F32)
    cbias = sb("cbias", [128, 2], F32)
    nc.const_aps.register(F32, -32.0, cbias[:, 0:1])
    nc.const_aps.register(F32, 1e-5, cbias[:, 1:2])

    TRI32 = consts32[:, 0:128]
    IDN32 = consts32[:, 128:256]
    TRIBF = constsbf[:, 0:128]
    IDNBF = constsbf[:, 128:256]
    H128BF = constsbf[:, 256:384]
    ONEROWBF = constsbf[0:1, 384:512]
    ONECOLBF = constsbf[:, 384:385]

    scopeB = ExitStack()
    xnT = sb("xnT", [128, 16 * TC], BF16, scope=scopeB, side="right")
    scopeWS = ExitStack()
    wstr = [sb(f"wstr{i}", [128, 2048], BF16, scope=scopeWS, side="right") for i in range(2)]
    tb_wstr = [mkbuf(f"wstr{i}") for i in range(2)]
    scopeS1 = ExitStack()
    p32f = sb("p32f", [128, 8 * TC], BF16, scope=scopeS1)
    kf = sb("kf", [128, 8 * TC], BF16, scope=scopeS1)
    qf = sb("qf", [128, 8 * TC], BF16, scope=scopeS1)
    vt = sb("vt", [128, 8 * 2048], BF16, scope=scopeS1)
    sg = sb("sg", [128, 8 * 2048], BF16, scope=scopeS1)
    tb_vt, tb_sg = mkbuf("vt"), mkbuf("sg")

    tb_consts = TBuf("consts")
    tb_xnT = TBuf("xnT")
    tb_xnTh = [TBuf("xnT0"), TBuf("xnT1")]
    tb_p32f = TBuf("p32f")
    tb_kf = TBuf("kf")
    tb_qf = TBuf("qf")

    P.phase = "ph0"
    P.op("sp", lambda e: e.dma_start(out=consts32[:, 0:128], in_=tri_in), writes=[tb_consts], dma=ch_const)
    P.op("sp", lambda e: e.dma_start(out=consts32[:, 128:256], in_=idn_in), writes=[tb_consts], dma=ch_const)
    P.op("sp", lambda e: e.dma_start(out=consts32[:, 256:384], in_=h128_in), writes=[tb_consts], dma=ch_const)
    P.op("sp", lambda e: e.dma_start(out=consts32[:, 384:512], in_=ones_in), writes=[tb_consts], dma=ch_const)
    P.op("sp", lambda e: e.dma_start(out=h64d_32[:], in_=h64d_in), writes=[tb_consts], dma=ch_const)
    P.op("sp", lambda e: e.dma_start(out=gnw_sb[:], in_=gnw_in), writes=[tb_consts], dma=ch_const)
    P.op("sp", lambda e: e.dma_start(out=pmask_sb[:], in_=pmask_in), writes=[tb_consts], dma=ch_const)
    P.op("ve", lambda e: e.tensor_copy(constsbf[:], consts32[:]), reads=[tb_consts], writes=[tb_consts])
    P.op("ve", lambda e: e.tensor_copy(h64d_bf[:], h64d_32[:]), reads=[tb_consts], writes=[tb_consts])
    P.op("ve", lambda e: e.memset(cbias[:, 0:1], -32.0), writes=[tb_consts])
    P.op("ve", lambda e: e.memset(cbias[:, 1:2], 1e-5), writes=[tb_consts])

    P.phase = "ph1_ln1"
    scopeA = ExitStack()
    xbuf = [sb(f"xbuf{i}", [128, D], F32, scope=scopeA) for i in range(2)]
    tb_x = [mkbuf(f"x{i}") for i in range(2)]
    lnstat = sb("lnstat", [128, 8], F32, scope=scopeA)
    tb_lnstat = mkbuf("lnstat")
    xn_tm = [sb(f"xn_tm{i}", [128, D], BF16, scope=scopeA) for i in range(2)]
    tb_xn_tm = [mkbuf(f"xn_tm{i}") for i in range(2)]
    sqjunk = sb("sqjunk", [128, D], BF16, scope=scopeA)
    tb_sqjunk = mkbuf("sqjunk")

    def emit_ln(xb, tb_xb, xnb, tb_xnb, lnstat, tb_lnstat, sqjunk, tb_sqjunk):
        P.op("ve", lambda e: e.reduce_sum(lnstat[:, 0:1], xb[:], axis=AX),
             reads=[tb_xb], writes=[tb_lnstat])
        P.op("se", lambda e: e.activation(sqjunk[:], xb[:], AF.Square,
                                          accum_out=lnstat[:, 1:2]),
             reads=[tb_xb], writes=[tb_lnstat, tb_sqjunk])
        P.op("ve", lambda e: e.tensor_scalar_mul(lnstat[:, 2:3], lnstat[:, 0:1], 1.0 / D),
             reads=[tb_lnstat], writes=[tb_lnstat])
        P.op("ve", lambda e: e.tensor_mul(lnstat[:, 3:4], lnstat[:, 2:3], lnstat[:, 2:3]),
             reads=[tb_lnstat], writes=[tb_lnstat])
        P.op("ve", lambda e: e.tensor_scalar(lnstat[:, 4:5], lnstat[:, 1:2], 1.0 / D,
                                             None, op0=ALU.mult),
             reads=[tb_lnstat], writes=[tb_lnstat])
        P.op("ve", lambda e: e.tensor_sub(lnstat[:, 4:5], lnstat[:, 4:5], lnstat[:, 3:4]),
             reads=[tb_lnstat], writes=[tb_lnstat])
        P.op("se", lambda e: e.activation(lnstat[:, 5:6], lnstat[:, 4:5], AF.Ln, bias=1e-5),
             reads=[tb_lnstat], writes=[tb_lnstat])
        P.op("se", lambda e: e.activation(lnstat[:, 6:7], lnstat[:, 5:6], AF.Exp, scale=-0.5),
             reads=[tb_lnstat], writes=[tb_lnstat])
        P.op("ve", lambda e: e.tensor_mul(lnstat[:, 7:8], lnstat[:, 2:3], lnstat[:, 6:7]),
             reads=[tb_lnstat], writes=[tb_lnstat])
        P.op("ve", lambda e: e.tensor_scalar_mul(lnstat[:, 7:8], lnstat[:, 7:8], -1.0),
             reads=[tb_lnstat], writes=[tb_lnstat])
        P.op("ve", lambda e: e.tensor_scalar(xnb[:], xb[:], lnstat[:, 6:7],
                                             lnstat[:, 7:8], op0=ALU.mult, op1=ALU.add),
             reads=[tb_xb, tb_lnstat], writes=[tb_xnb])

    for i in range(NCH):
        xb, tb_xb = xbuf[i % 2], tb_x[i % 2]
        xnb, tb_xnb = xn_tm[i % 2], tb_xn_tm[i % 2]
        P.op("sp", lambda e, i=i, xb=xb: e.dma_start(
            out=xb[:], in_=x_in[i * 128:(i + 1) * 128, :]),
            writes=[tb_xb], dma=ch_x[i % 2])
        emit_ln(xb, tb_xb, xnb, tb_xnb, lnstat, tb_lnstat, sqjunk, tb_sqjunk)
        for dt in range(16):
            bank, tbk = (psF, tF) if dt % 2 == 0 else (psG, tG)
            w = bank[:, (dt // 2) % 4 * 128:((dt // 2) % 4 + 1) * 128]
            P.op("te", lambda e, dt=dt, xnb=xnb, w=w: e.transpose(
                w, xnb[:, dt * 128:(dt + 1) * 128], IDNBF),
                reads=[tb_xnb, tb_consts], writes=[tbk])
            P.op("ve", lambda e, dt=dt, i=i, w=w: e.tensor_copy(
                xnT[:, dt * TC + i * 128: dt * TC + (i + 1) * 128], w),
                reads=[tbk], writes=[tb_xnTh[i // 4]], appends=[tb_xnT])

    P.phase = "ph2_skq"
    slots = [((psE[:, 0:512], psE[:, 512:1024]), (tE,)),
             ((psA[:, 0:512], psB[:, 0:512]), (tA, tB)),
             ((psC[:, 0:512], psD[:, 0:512]), (tC_, tD))]

    wi = 0
    for pname, wsrc, dst, tb_dst in [("s", wsS, p32f, tb_p32f),
                                     ("k", wkS, kf, tb_kf)]:
        for ct in range(8):
            wb, tb_wb = wstr[wi % 2], tb_wstr[wi % 2]
            (h0, h1), tbs = slots[wi % 3]
            wi += 1
            P.op("gp", lambda e, wsrc=wsrc, ct=ct, wb=wb: e.dma_start(
                out=wb[:], in_=wsrc[ct]), writes=[tb_wb], dma=ch_w[(wi - 1) % 3])
            for dk in range(16):
                for ns, half in enumerate((h0, h1)):
                    P.op("te", lambda e, dk=dk, ns=ns, wb=wb, half=half:
                         e.matmul(half,
                                  wb[:, dk * 128:(dk + 1) * 128],
                                  xnT[:, dk * TC + ns * 512: dk * TC + (ns + 1) * 512],
                                  start=(dk == 0), stop=(dk == 15)),
                         reads=[tb_wb, tb_xnTh[ns]], writes=list(tbs))
            o = ct * TC
            if pname == "s":
                for ns, half in enumerate((h0, h1)):
                    P.op("se", lambda e, o=o, ns=ns, half=half: e.activation(
                        p32f[:, o + ns * 512: o + (ns + 1) * 512], half, AF.Exp, bias=-32.0),
                        reads=list(tbs), writes=[tb_p32f])
                P.op("ve", lambda e, o=o: e.tensor_scalar(
                    p32f[:, o:o + TC], p32f[:, o:o + TC], E64, 1.0,
                    op0=ALU.max, op1=ALU.min),
                    reads=[tb_p32f], writes=[tb_p32f])
            elif pname == "k":
                for ns, half in enumerate((h0, h1)):
                    P.op("ve", lambda e, o=o, ns=ns, half=half: e.tensor_copy(
                        kf[:, o + ns * 512: o + (ns + 1) * 512], half),
                        reads=list(tbs), writes=[tb_kf])
            else:
                for ns, half in enumerate((h0, h1)):
                    P.op("se", lambda e, o=o, ns=ns, half=half: e.activation(
                        qf[:, o + ns * 512: o + (ns + 1) * 512], half, AF.Copy, scale=0.125),
                        reads=list(tbs), writes=[tb_qf])

    for nm, src, tb_src in [("xnT", xnT, tb_xnT), ("p32f", p32f, tb_p32f),
                            ("kf", kf, tb_kf), ("qf", qf, tb_qf)]:
        ap = dumpt(nm, [128, 8192 if nm != "xnT" else 16384], BF16)
        if ap is not None:
            P.op("sp", lambda e, ap=ap, src=src: e.dma_start(out=ap, in_=src[:]),
                 reads=[tb_src], dma=ch_dump)

    free_scope(scopeA, tb_x + [tb_lnstat, tb_sqjunk] + tb_xn_tm + tb_wstr)

    P.phase = "ph2b_vg"
    if phases >= 3:
        scopeC = ExitStack()
        wbig = sb("wbig", [128, 16384], BF16, scope=scopeC, side="right")
        tb_wbig = mkbuf("sgw")
        for pname, wsrc, dst, tb_dst in [("v", wvS, vt, tb_vt)]:
            for vh in range(2):
                P.op("gp", lambda e, wsrc=wsrc, vh=vh: e.dma_start(
                    out=wbig[:], in_=wsrc[:, vh, :]), writes=[tb_wbig], dma=ch_wbig)
                for i in range(NCH):
                    (h0, h1), tbs = slots[wi % 3]
                    wi += 1
                    for dk in range(16):
                        for ns, half in enumerate((h0, h1)):
                            P.op("te", lambda e, dk=dk, ns=ns, i=i, half=half:
                                 e.matmul(half,
                                          xnT[:, dk * TC + i * 128:dk * TC + (i + 1) * 128],
                                          wbig[:, dk * 1024 + ns * 512: dk * 1024 + (ns + 1) * 512],
                                          start=(dk == 0), stop=(dk == 15)),
                                 reads=[tb_wbig, tb_xnTh[i // 4]], writes=list(tbs))
                    o = i * 2048 + vh * 1024
                    for ns, half in enumerate((h0, h1)):
                        if pname == "v":
                            P.op("ve", lambda e, o=o, ns=ns, half=half: e.tensor_copy(
                                vt[:, o + ns * 512: o + (ns + 1) * 512], half),
                                reads=list(tbs), writes=[tb_vt])
                        else:
                            P.op("se", lambda e, o=o, ns=ns, half=half: e.activation(
                                sg[:, o + ns * 512: o + (ns + 1) * 512], half, AF.Silu),
                                reads=list(tbs), writes=[tb_sg])
        for nm, src, tb_src in [("vt", vt, tb_vt), ("sg", sg, tb_sg)]:
            ap = dumpt(nm, [128, 16384], BF16)
            if ap is not None:
                P.op("sp", lambda e, ap=ap, src=src: e.dma_start(out=ap, in_=src[:]),
                     reads=[tb_src], dma=ch_dump)

    P.phase = "ph3_stage1"
    if phases >= 4:
        scopeAcc = ExitStack()
        acc = sb("acc", [128, 2560], F32, scope=scopeAcc)
        tb_acc = mkbuf("acc")
        scopeS2 = ExitStack()
        p32t_sb = [sb(f"p32t{i}", [128, 128], BF16, scope=scopeS2) for i in range(3)]
        kt_sb = [sb(f"kt{i}", [128, 128], BF16, scope=scopeS2) for i in range(3)]
        tb_p32t = [mkbuf(f"p32t{i}") for i in range(3)]
        tb_kt = [mkbuf(f"kt{i}") for i in range(3)]
        A1_sb = acc[0:1, 1536:2560]
        tb_A1 = tb_acc
        ccstage, tb_ccstage = acc, tb_acc
        tb_ccin = mkbuf("ccin")
        tb_ccout = mkbuf("ccout")
        scopeR = ExitStack()
        rkbuf = [sb(f"rk{i}", [128, 640], F32, scope=scopeR) for i in range(2)]
        tb_rk = [mkbuf(f"rk{i}") for i in range(2)]

        P.op("ve", lambda e: e.memset(acc[:, 1536:2560], 0.0), writes=[tb_acc])

        def emit_tr_pair(c, hp):
            bank, tbk = (psF, tF) if hp % 2 == 0 else (psG, tG)
            pw = bank[:, 0:128]
            kw = bank[:, 128:256]
            P.op("te", lambda e, c=c, hp=hp, pw=pw: e.transpose(
                pw, p32f[:, hp * TC + c * 128: hp * TC + (c + 1) * 128], IDNBF),
                reads=[tb_p32f, tb_consts], writes=[tbk])
            P.op("te", lambda e, c=c, hp=hp, kw=kw: e.transpose(
                kw, kf[:, hp * TC + c * 128: hp * TC + (c + 1) * 128], IDNBF),
                reads=[tb_kf, tb_consts], writes=[tbk])
            P.op("se", lambda e, hp=hp, pw=pw: e.activation(p32t_sb[hp % 3][:], pw, AF.Copy),
                 reads=[tbk], writes=[tb_p32t[hp % 3]])
            P.op("se", lambda e, hp=hp, kw=kw: e.activation(kt_sb[hp % 3][:], kw, AF.Copy),
                 reads=[tbk], writes=[tb_kt[hp % 3]])
            return bank, tbk

        for c in range(NCH):
            for hp in range(8):
                emit_tr_pair(c, hp)
                for hh in range(2):
                    h = 2 * hp + hh
                    P.op("te", lambda e, c=c, hp=hp, hh=hh: e.matmul(
                        psB[hh * 64:(hh + 1) * 64, hp * 64:(hp + 1) * 64],
                        kt_sb[hp % 3][:, hh * 64:(hh + 1) * 64],
                        p32t_sb[hp % 3][:, hh * 64:(hh + 1) * 64],
                        start=True, stop=True,
                        tile_position=(0, hh * 64)),
                        reads=[tb_kt[hp % 3], tb_p32t[hp % 3]], writes=[tB])
                    P.op("te", lambda e, c=c, hp=hp, hh=hh, h=h: e.matmul(
                        psE[hh * 64:(hh + 1) * 64, hp * 128:(hp + 1) * 128],
                        p32t_sb[hp % 3][:, hh * 64:(hh + 1) * 64],
                        vt[:, c * 2048 + h * 128: c * 2048 + (h + 1) * 128],
                        start=True, stop=True,
                        tile_position=(0, hh * 64)),
                        reads=[tb_p32t[hp % 3], tb_vt], writes=[tE])
                csb, tcs = (psA, tA) if hp % 2 == 0 else (psC, tC_)
                P.op("te", lambda e, hp=hp, csb=csb: e.matmul(
                    csb[0:1, 0:128], ONECOLBF, p32t_sb[hp % 3][:], start=True, stop=True),
                    reads=[tb_p32t[hp % 3], tb_consts], writes=[tcs])
                if c == 0:
                    P.op("ve", lambda e, hp=hp, csb=csb: e.tensor_copy(
                        acc[0:1, 1536 + hp * 128:1536 + (hp + 1) * 128], csb[0:1, 0:128]),
                        reads=[tcs], writes=[tb_A1])
                else:
                    P.op("ve", lambda e, hp=hp, csb=csb: e.tensor_add(
                        acc[0:1, 1536 + hp * 128:1536 + (hp + 1) * 128],
                        acc[0:1, 1536 + hp * 128:1536 + (hp + 1) * 128], csb[0:1, 0:128]),
                        reads=[tcs, tb_A1], writes=[tb_A1])
            if c == 0:
                P.op("ve", lambda e: e.tensor_copy(ccstage[:, 0:512], psB[:]),
                     reads=[tB], writes=[tb_ccstage])
                P.op("ve", lambda e: e.tensor_copy(ccstage[:, 512:1536], psE[:]),
                     reads=[tB, tE], writes=[tb_ccstage])
            else:
                opB = P.op("ve", lambda e: e.tensor_add(ccstage[:, 0:512], ccstage[:, 0:512], psB[:]),
                     reads=[tB, tb_ccstage], writes=[tb_ccstage])
                opE = P.op("ve", lambda e: e.tensor_add(ccstage[:, 512:1536], ccstage[:, 512:1536], psE[:]),
                     reads=[tE, tb_ccstage], writes=[tb_ccstage])
                if c == NCH - 1:
                    last_psB_reader, last_psE_reader = opB, opE

        P.phase = "ph3b_cc"
        P.op("gp", lambda e: e.dma_start(out=cc_in[:, :], in_=ccstage[:]),
             reads=[tb_ccstage], writes=[tb_ccin], dma=ch_cc)
        P.op("gp", lambda e: e.collective_compute(
            "AllGather", ALU.bypass, ins=[cc_in.ap().opt()], outs=[cc_out.ap().opt()],
            replica_groups=[list(range(NCORE))]),
            reads=[tb_ccin], writes=[tb_ccout], cc=True)

        P.phase = "ph3c_gq"
        for pname, wsrc, dst, tb_dst in [("g", wgS, sg, tb_sg)]:
            for vh in range(2):
                P.op("gp", lambda e, wsrc=wsrc, vh=vh: e.dma_start(
                    out=wbig[:], in_=wsrc[:, vh, :]), writes=[tb_wbig], dma=ch_wbig)
                for i in range(NCH):
                    (h0, h1), tbs = slots[wi % 3]
                    wi += 1
                    for dk in range(16):
                        for ns, half in enumerate((h0, h1)):
                            P.op("te", lambda e, dk=dk, ns=ns, i=i, half=half:
                                 e.matmul(half,
                                          xnT[:, dk * TC + i * 128:dk * TC + (i + 1) * 128],
                                          wbig[:, dk * 1024 + ns * 512: dk * 1024 + (ns + 1) * 512],
                                          start=(dk == 0), stop=(dk == 15)),
                                 reads=[tb_wbig, tb_xnTh[i // 4]], writes=list(tbs))
                    o = i * 2048 + vh * 1024
                    for ns, half in enumerate((h0, h1)):
                        P.op("se", lambda e, o=o, ns=ns, half=half: e.activation(
                            sg[:, o + ns * 512: o + (ns + 1) * 512], half, AF.Silu),
                            reads=list(tbs), writes=[tb_sg])
        free_scope(scopeC, [tb_wbig])
        for pname, wsrc, dst, tb_dst in [("q", wqS, qf, tb_qf)]:
            for ct in range(8):
                wb, tb_wb = wstr[wi % 2], tb_wstr[wi % 2]
                (h0, h1), tbs = slots[wi % 3]
                wi += 1
                P.op("gp", lambda e, wsrc=wsrc, ct=ct, wb=wb: e.dma_start(
                    out=wb[:], in_=wsrc[ct]), writes=[tb_wb], dma=ch_w[(wi - 1) % 3])
                for dk in range(16):
                    for ns, half in enumerate((h0, h1)):
                        P.op("te", lambda e, dk=dk, ns=ns, wb=wb, half=half:
                             e.matmul(half,
                                      wb[:, dk * 128:(dk + 1) * 128],
                                      xnT[:, dk * TC + ns * 512: dk * TC + (ns + 1) * 512],
                                      start=(dk == 0), stop=(dk == 15)),
                             reads=[tb_wb, tb_xnT], writes=list(tbs))
                o = ct * TC
                for ns, half in enumerate((h0, h1)):
                    P.op("se", lambda e, o=o, ns=ns, half=half: e.activation(
                        qf[:, o + ns * 512: o + (ns + 1) * 512], half, AF.Copy, scale=0.125),
                        reads=list(tbs), writes=[tb_qf])
        free_scope(scopeWS, tb_wstr)
        P.phase = "ph3d_msum"
        P.op("ve", lambda e: e.memset(acc[:], 0.0), reads=[tb_ccin], writes=[tb_acc])
        for j in range(NCORE):
            for hf in range(4):
                rb, tb_rb = rkbuf[hf % 2], tb_rk[hf % 2]
                P.op("gp", lambda e, j=j, hf=hf, rb=rb: e.dma_start(
                    out=rb[:], in_=cc_out[j * 128:(j + 1) * 128,
                                          hf * 640:(hf + 1) * 640]),
                    reads=[tb_ccout], writes=[tb_rb], dma=ch_rk[hf % 2])
                P.op("ve", lambda e, j=j, hf=hf, rb=rb: e.scalar_tensor_tensor(
                    acc[:, hf * 640:(hf + 1) * 640], rb[:], pmask_sb[:, j:j + 1],
                    acc[:, hf * 640:(hf + 1) * 640],
                    op0=ALU.mult, op1=ALU.add),
                    reads=[tb_rb, tb_consts, tb_acc], writes=[tb_acc])
        ap = dumpt("acc", [128, 2560], F32)
        if ap is not None:
            P.op("sp", lambda e, ap=ap: e.dma_start(out=ap, in_=acc[:]),
                 reads=[tb_acc], dma=ch_dump)
        free_scope(scopeR, [tb_ccstage] + tb_rk)
        free_scope(scopeB, [tb_xnT] + tb_xnTh)
        A_sbbf = sb("A_sbbf", [1, 1024], BF16, scope=scopeS2)
        tb_Abf = mkbuf("A_sbbf")
        hk_sb = sb("hk_sb", [128, 512], BF16, scope=scopeS2)
        hv_sb = sb("hv_sb", [128, 1024], BF16, scope=scopeS2)
        tb_hk, tb_hv = mkbuf("hk_sb"), mkbuf("hv_sb")
        hk32_sb = acc[:, 0:512]
        hv32_sb = acc[:, 512:1536]
        A_sb = acc[0:1, 1536:2560]
        tb_hk32 = tb_hv32 = tb_As = tb_acc

    P.phase = "ph4_scan"
    if phases >= 5:
        ch_pc = P.chan("preconv")
        tb_woB = mkbuf("woB")
        tb_w2B = mkbuf("w2B")
        for q in range(4):
            P.op("gp", lambda e, q=q: e.dma_start(
                out=woB[:, q * 8192:(q + 1) * 8192],
                in_=woS[:, q * 8192:(q + 1) * 8192]),
                writes=[tb_woB], dma=ch_pc)
        scopeY = ExitStack()
        ynT = sb("ynT", [128, 16 * TC], BF16, scope=scopeY, side="right")
        tb_ynT = mkbuf("ynT")
        P.op("ve", lambda e: e.tensor_copy(A_sbbf[:], A_sb),
             reads=[tb_As], writes=[tb_Abf])
        scopeO = ExitStack()
        of_sb = sb("of_sb", [128, 16 * TC], BF16, scope=scopeO, side="right")
        tb_of = mkbuf("of")
        u_pair = [sb(f"u_pair{i}", [128, 128], BF16, scope=scopeS2) for i in range(3)]
        tb_up = [mkbuf(f"up{i}") for i in range(3)]
        uf_sb = [sb(f"uf{i}", [128, 128], BF16, scope=scopeS2) for i in range(3)]
        tb_uf = [mkbuf(f"uf{i}") for i in range(3)]
        s1t_sb = [sb(f"s1t{i}", [128, 128], BF16, scope=scopeS2) for i in range(4)]
        tb_s1t = [mkbuf(f"s1t{i}") for i in range(4)]
        wov_sb = [sb(f"wov{i}", [128, 128], BF16, scope=scopeS2) for i in range(4)]
        tb_wov = [mkbuf(f"wov{i}") for i in range(4)]
        rc_sb = [sb(f"rc{i}", [128, 64], F32, scope=scopeS2) for i in range(4)]
        ok_sb = [sb(f"okb{i}", [128, 64], F32, scope=scopeS2) for i in range(4)]
        e_sb = [sb(f"eb{i}", [128, 64], F32, scope=scopeS2) for i in range(4)]
        st_sb = [sb(f"stb{i}", [128, 4], F32, scope=scopeS2) for i in range(4)]
        st2_sb = [sb(f"st2b{i}", [128, 4], F32, scope=scopeS2) for i in range(4)]
        og_sb = [sb(f"og{i}", [128, 128], BF16, scope=scopeS2) for i in range(4)]
        sqj2 = [sb("sqj2_0", [128, 128], BF16, scope=scopeS2)] * 4
        tb_rc = [mkbuf(f"rc{i}") for i in range(4)]
        tb_ok = [mkbuf(f"ok{i}") for i in range(4)]
        tb_e = [mkbuf(f"e{i}") for i in range(4)]
        tb_st = [mkbuf(f"st{i}") for i in range(4)]
        tb_st2 = [mkbuf(f"st2{i}") for i in range(4)]
        tb_og = [mkbuf(f"og{i}") for i in range(4)]
        tb_sqj2 = [mkbuf("sqj0")] * 4

        pools = [(psA, tA), (psC, tC_), (psD, tD)]
        tB_hp, tE_hp = [], []
        for hp in range(8):
            t1 = TBuf(f"psBs{hp}")
            t1.r = [last_psB_reader]
            t2 = TBuf(f"psEs{hp}")
            t2.r = [last_psE_reader]
            tB_hp.append(t1)
            tE_hp.append(t2)
        for c in range(NCH):
            P.op("ve", lambda e: e.tensor_copy(hk_sb[:], hk32_sb),
                 reads=[tb_hk32], writes=[tb_hk])
            P.op("ve", lambda e: e.tensor_copy(hv_sb[:], hv32_sb),
                 reads=[tb_hv32], writes=[tb_hv])
            if c > 0:
                P.op("ve", lambda e: e.tensor_copy(A_sbbf[:], A_sb),
                     reads=[tb_As], writes=[tb_Abf])
            for hp in range(8):
                bank, tbk = emit_tr_pair(c, hp)
                UFW = bank[:, 256:384]
                OTW = [bank[:, 384:512], bank[:, 512:640]]
                hs = []
                for hh in range(2):
                    g = hp * 2 + hh
                    pool, tpool = pools[g % 3]
                    hs.append(dict(h=g, hh=hh, g=g, pool=pool, tp=tpool,
                                   sl=slice(hh * 64, (hh + 1) * 64),
                                   toff=hp * TC + c * 128))
                pt, tb_pt = p32t_sb[hp % 3], tb_p32t[hp % 3]
                kt, tb_ktb = kt_sb[hp % 3], tb_kt[hp % 3]
                for s in hs:
                    P.op("te", lambda e, s=s, hp=hp, pt=pt, kt=kt: e.matmul(
                        psB[s["sl"], hp * 64:(hp + 1) * 64],
                        kt[:, s["sl"]], pt[:, s["sl"]],
                        start=True, stop=True, tile_position=(0, s["hh"] * 64)),
                        reads=[tb_ktb, tb_pt], writes=[tB_hp[hp]])
                    P.op("te", lambda e, s=s, hp=hp, c=c, pt=pt: e.matmul(
                        psE[s["sl"], hp * 128:(hp + 1) * 128],
                        pt[:, s["sl"]],
                        vt[:, c * 2048 + s["h"] * 128: c * 2048 + (s["h"] + 1) * 128],
                        start=True, stop=True, tile_position=(0, s["hh"] * 64)),
                        reads=[tb_pt, tb_vt], writes=[tE_hp[hp]])
                P.op("ve", lambda e, hp=hp: e.tensor_add(
                    hk32_sb[:, hp * 64:(hp + 1) * 64],
                    hk32_sb[:, hp * 64:(hp + 1) * 64],
                    psB[:, hp * 64:(hp + 1) * 64]),
                    reads=[tB_hp[hp], tb_hk32], writes=[tb_hk32])
                P.op("ve", lambda e, hp=hp: e.tensor_add(
                    hv32_sb[:, hp * 128:(hp + 1) * 128],
                    hv32_sb[:, hp * 128:(hp + 1) * 128],
                    psE[:, hp * 128:(hp + 1) * 128]),
                    reads=[tE_hp[hp], tb_hv32], writes=[tb_hv32])
                for s in hs:
                    P.op("te", lambda e, s=s: e.matmul(
                        s["pool"][:, 0:128],
                        kf[s["sl"], s["toff"]:s["toff"] + 128],
                        qf[s["sl"], s["toff"]:s["toff"] + 128],
                        start=True, stop=True, tile_position=(s["hh"] * 64, 0)),
                        reads=[tb_kf, tb_qf], writes=[s["tp"]])
                for s in hs:
                    P.op("ve", lambda e, s=s: e.tensor_mul(
                        s1t_sb[s["g"] % 4][:], s["pool"][:, 0:128], TRI32),
                        reads=[s["tp"], tb_consts], writes=[tb_s1t[s["g"] % 4]])
                for s in hs:
                    P.op("te", lambda e, s=s, pt=pt: e.matmul(
                        s["pool"][0:1, 192:256], ONECOLBF, pt[:, s["sl"]],
                        start=True, stop=True),
                        reads=[tb_pt, tb_consts], writes=[s["tp"]])
                for s in hs:
                    P.op("ve", lambda e, s=s: e.tensor_add(
                        acc[0:1, 1536 + s["h"] * 64:1536 + (s["h"] + 1) * 64],
                        acc[0:1, 1536 + s["h"] * 64:1536 + (s["h"] + 1) * 64],
                        s["pool"][0:1, 192:256]),
                        reads=[s["tp"], tb_As], writes=[tb_As])
                for s in hs:
                    P.op("te", lambda e, s=s, pt=pt: e.matmul(
                        s["pool"][:, 128:192], TRIBF, pt[:, s["sl"]],
                        start=True, stop=False),
                        reads=[tb_pt, tb_consts], writes=[s["tp"]])
                    P.op("te", lambda e, s=s: e.matmul(
                        s["pool"][:, 128:192], ONEROWBF,
                        A_sbbf[0:1, s["h"] * 64:(s["h"] + 1) * 64],
                        start=False, stop=True),
                        reads=[tb_Abf, tb_consts], writes=[s["tp"]])
                for s in hs:
                    P.op("te", lambda e, s=s, hp=hp: e.matmul(
                        s["pool"][:, 0:64],
                        qf[s["sl"], s["toff"]:s["toff"] + 128],
                        hk_sb[s["sl"], hp * 64:(hp + 1) * 64],
                        start=True, stop=False, tile_position=(s["hh"] * 64, 0)),
                        reads=[tb_qf, tb_hk], writes=[s["tp"]])
                    P.op("te", lambda e, s=s, pt=pt: e.matmul(
                        s["pool"][:, 0:64], s1t_sb[s["g"] % 4][:], pt[:, s["sl"]],
                        start=False, stop=True),
                        reads=[tb_s1t[s["g"] % 4], tb_pt], writes=[s["tp"]])
                for s in hs:
                    P.op("ve", lambda e, s=s: e.reciprocal(
                        rc_sb[s["g"] % 4][:], s["pool"][:, 128:192]),
                        reads=[s["tp"]], writes=[tb_rc[s["g"] % 4]])
                    P.op("ve", lambda e, s=s: e.tensor_mul(
                        ok_sb[s["g"] % 4][:], s["pool"][:, 0:64], rc_sb[s["g"] % 4][:]),
                        reads=[s["tp"], tb_rc[s["g"] % 4]], writes=[tb_ok[s["g"] % 4]])
                    P.op("ve", lambda e, s=s: e.reduce_max(
                        st_sb[s["g"] % 4][:, 0:1], ok_sb[s["g"] % 4][:], axis=AX, negate=True),
                        reads=[tb_ok[s["g"] % 4]], writes=[tb_st[s["g"] % 4]])
                for s in hs:
                    P.op("se", lambda e, s=s: e.activation(
                        e_sb[s["g"] % 4][:], ok_sb[s["g"] % 4][:], AF.Exp,
                        bias=st_sb[s["g"] % 4][:, 0:1]),
                        reads=[tb_ok[s["g"] % 4], tb_st[s["g"] % 4]], writes=[tb_e[s["g"] % 4]])
                for s in hs:
                    P.op("ve", lambda e, s=s: e.reduce_sum(
                        st_sb[s["g"] % 4][:, 1:2], e_sb[s["g"] % 4][:], axis=AX),
                        reads=[tb_e[s["g"] % 4]], writes=[tb_st[s["g"] % 4]])
                    P.op("ve", lambda e, s=s: e.reciprocal(
                        st_sb[s["g"] % 4][:, 2:3], st_sb[s["g"] % 4][:, 1:2]),
                        reads=[tb_st[s["g"] % 4]], writes=[tb_st[s["g"] % 4]])
                    P.op("ve", lambda e, s=s, hp=hp: e.scalar_tensor_tensor(
                        u_pair[hp % 3][:, s["hh"] * 64:(s["hh"] + 1) * 64],
                        e_sb[s["g"] % 4][:], st_sb[s["g"] % 4][:, 2:3], rc_sb[s["g"] % 4][:],
                        op0=ALU.mult, op1=ALU.mult),
                        reads=[tb_e[s["g"] % 4], tb_st[s["g"] % 4], tb_rc[s["g"] % 4]],
                        writes=[tb_up[hp % 3]])
                P.op("te", lambda e, hp=hp, UFW=UFW: e.transpose(
                    UFW, u_pair[hp % 3][:], IDNBF),
                    reads=[tb_up[hp % 3], tb_consts], writes=[tbk])
                P.op("se", lambda e, hp=hp, UFW=UFW: e.activation(uf_sb[hp % 3][:], UFW, AF.Copy),
                     reads=[tbk], writes=[tb_uf[hp % 3]])
                for s in hs:
                    P.op("te", lambda e, s=s, hp=hp: e.matmul(
                        s["pool"][:, 256:384],
                        p32f[s["sl"], s["toff"]:s["toff"] + 128],
                        uf_sb[hp % 3][s["sl"], :],
                        start=True, stop=True, tile_position=(s["hh"] * 64, 0)),
                        reads=[tb_p32f, tb_uf[hp % 3]], writes=[s["tp"]])
                for s in hs:
                    P.op("ve", lambda e, s=s: e.tensor_mul(
                        wov_sb[s["g"] % 4][:], s["pool"][:, 256:384], TRI32),
                        reads=[s["tp"], tb_consts], writes=[tb_wov[s["g"] % 4]])
                for s in hs:
                    P.op("te", lambda e, s=s, hp=hp: e.matmul(
                        s["pool"][:, 384:512], uf_sb[hp % 3][s["sl"], :],
                        hv_sb[s["sl"], hp * 128:(hp + 1) * 128],
                        start=True, stop=False, tile_position=(s["hh"] * 64, 0)),
                        reads=[tb_uf[hp % 3], tb_hv], writes=[s["tp"]])
                    P.op("te", lambda e, s=s, c=c: e.matmul(
                        s["pool"][:, 384:512], wov_sb[s["g"] % 4][:],
                        vt[:, c * 2048 + s["h"] * 128: c * 2048 + (s["h"] + 1) * 128],
                        start=False, stop=True),
                        reads=[tb_wov[s["g"] % 4], tb_vt], writes=[s["tp"]])
                for s in hs:
                    P.op("se", lambda e, s=s: e.activation(
                        sqj2[s["g"] % 4][:], s["pool"][:, 384:512], AF.Square,
                        accum_out=st2_sb[s["g"] % 4][:, 0:1]),
                        reads=[s["tp"]], writes=[tb_st2[s["g"] % 4], tb_sqj2[s["g"] % 4]])
                    P.op("se", lambda e, s=s: e.activation(
                        st2_sb[s["g"] % 4][:, 1:2], st2_sb[s["g"] % 4][:, 0:1], AF.Ln,
                        bias=1e-5, scale=1.0 / 128.0),
                        reads=[tb_st2[s["g"] % 4]], writes=[tb_st2[s["g"] % 4]])
                    P.op("se", lambda e, s=s: e.activation(
                        st2_sb[s["g"] % 4][:, 2:3], st2_sb[s["g"] % 4][:, 1:2],
                        AF.Exp, scale=-0.5),
                        reads=[tb_st2[s["g"] % 4]], writes=[tb_st2[s["g"] % 4]])
                for s in hs:
                    P.op("ve", lambda e, s=s, c=c: e.scalar_tensor_tensor(
                        og_sb[s["g"] % 4][:], s["pool"][:, 384:512], st2_sb[s["g"] % 4][:, 2:3],
                        sg[:, c * 2048 + s["h"] * 128: c * 2048 + (s["h"] + 1) * 128],
                        op0=ALU.mult, op1=ALU.mult),
                        reads=[s["tp"], tb_st2[s["g"] % 4], tb_sg], writes=[tb_og[s["g"] % 4]])
                for s in hs:
                    P.op("te", lambda e, s=s, OTW=OTW: e.transpose(
                        OTW[s["hh"]], og_sb[s["g"] % 4][:], IDNBF),
                        reads=[tb_og[s["g"] % 4], tb_consts], writes=[tbk])
                for s in hs:
                    P.op("ve", lambda e, s=s, c=c, OTW=OTW: e.tensor_copy(
                        of_sb[:, s["h"] * TC + c * 128: s["h"] * TC + (c + 1) * 128],
                        OTW[s["hh"]]),
                        reads=[tbk], writes=[tb_of])

        ap = dumpt("of", [128, 16 * TC], BF16)
        if ap is not None:
            P.op("sp", lambda e, ap=ap: e.dma_start(out=ap, in_=of_sb[:]),
                 reads=[tb_of], dma=ch_dump)

    P.phase = "ph5_wo"
    if phases >= 6:
        free_scope(scopeS2, [tb_p32t[0], tb_p32t[1], tb_kt[0], tb_kt[1], tb_A1,
                             tb_As, tb_Abf, tb_hk, tb_hv, tb_hk32, tb_hv32]
                   + tb_up + tb_uf + tb_s1t + tb_wov + tb_rc + tb_ok + tb_e
                   + tb_st + tb_st2 + tb_og + tb_sqj2)
        free_scope(scopeAcc, [tb_acc])
        free_scope(scopeS1, [tb_p32f, tb_kf, tb_qf, tb_vt, tb_sg])
        ch_wo = P.chan("wo")
        ch_xw = [P.chan(f"xw{i}") for i in range(2)]
        ch_scx = [P.chan(f"scx{i}") for i in range(2)]
        scopeW = ExitStack()
        wo_sb = sb("wo_sb", [128, 32768], BF16, scope=scopeW)
        tb_wo = mkbuf("wo")
        xb2 = [sb(f"xb2_{i}", [128, D], F32, scope=scopeW) for i in range(2)]
        tb_xb2 = [mkbuf(f"xb2_{i}") for i in range(2)]
        x1b = [sb(f"x1b_{i}", [128, D], F32, scope=scopeW) for i in range(2)]
        tb_x1b = [mkbuf(f"x1b_{i}") for i in range(2)]
        yn2 = [sb(f"yn2_{i}", [128, D], BF16, scope=scopeW) for i in range(2)]
        tb_yn2 = [mkbuf(f"yn2_{i}") for i in range(2)]
        ln2st = sb("ln2st", [128, 8], F32, scope=scopeW)
        tb_ln2st = mkbuf("ln2st")
        sqj3 = sb("sqj3", [128, D], BF16, scope=scopeW)
        tb_sqj3 = mkbuf("sqj3")

        ch_wo4 = [P.chan(f"wo4_{i}") for i in range(4)]
        tb_wo_sl = [mkbuf(f"wo_sl{i}") for i in range(4)]
        for q, eng in enumerate(("sp", "gp", "se", "gp")):
            P.op(eng, lambda e, q=q: e.dma_start(
                out=wo_sb[:, q * 8192:(q + 1) * 8192],
                in_=woB[:, q * 8192:(q + 1) * 8192]),
                reads=[tb_woB], writes=[tb_wo_sl[q]], dma=ch_wo4[q])
        P.op("se", lambda e: e.activation(wo_sb[:], wo_sb[:], AF.Copy, scale=gnw_sb[:, 0:1]),
             reads=[tb_consts] + tb_wo_sl, writes=[tb_wo])

        for i in range(NCH):
            P.op("gp", lambda e, i=i: e.dma_start(
                out=xb2[i % 2][:], in_=x_in[i * 128:(i + 1) * 128, :]),
                writes=[tb_xb2[i % 2]], dma=ch_xw[i % 2])
            for dh in range(2):
                (h0, h1), tbs = slots[wi % 3]
                wi += 1
                for vk in range(16):
                    for ns, half in enumerate((h0, h1)):
                        P.op("te", lambda e, i=i, dh=dh, vk=vk, ns=ns, half=half:
                             e.matmul(half,
                                      of_sb[:, vk * TC + i * 128: vk * TC + (i + 1) * 128],
                                      wo_sb[:, vk * 2048 + dh * 1024 + ns * 512:
                                            vk * 2048 + dh * 1024 + (ns + 1) * 512],
                                      start=(vk == 0), stop=(vk == 15)),
                             reads=[tb_of, tb_wo], writes=list(tbs))
                for ns, half in enumerate((h0, h1)):
                    off = dh * 1024 + ns * 512
                    P.op("ve", lambda e, i=i, off=off, half=half: e.tensor_add(
                        x1b[i % 2][:, off:off + 512], half, xb2[i % 2][:, off:off + 512]),
                        reads=list(tbs) + [tb_xb2[i % 2]], writes=[tb_x1b[i % 2]])
            P.op("gp", lambda e, i=i: e.dma_start(
                out=scrX1[i * 128:(i + 1) * 128, :], in_=x1b[i % 2][:]),
                reads=[tb_x1b[i % 2]], appends=[tb_scrX1], dma=ch_scx[i % 2])
            emit_ln(x1b[i % 2], tb_x1b[i % 2], yn2[i % 2], tb_yn2[i % 2],
                    ln2st, tb_ln2st, sqj3, tb_sqj3)
            for dt in range(16):
                bank, tbk = (psF, tF) if dt % 2 == 0 else (psG, tG)
                w = bank[:, (dt // 2) % 4 * 128:((dt // 2) % 4 + 1) * 128]
                P.op("te", lambda e, dt=dt, i=i, w=w: e.transpose(
                    w, yn2[i % 2][:, dt * 128:(dt + 1) * 128], IDNBF),
                    reads=[tb_yn2[i % 2], tb_consts], writes=[tbk])
                P.op("ve", lambda e, dt=dt, i=i, w=w: e.tensor_copy(
                    ynT[:, dt * TC + i * 128: dt * TC + (i + 1) * 128], w),
                    reads=[tbk], writes=[tb_ynT])
        ap = dumpt("x1", [TC, 2048], F32)
        if ap is not None:
            P.op("sp", lambda e, ap=ap: e.dma_start(out=ap, in_=scrX1),
                 reads=[tb_scrX1], dma=ch_dump)
        ap = dumpt("ynT", [128, 16 * TC], BF16)
        if ap is not None:
            P.op("sp", lambda e, ap=ap: e.dma_start(out=ap, in_=ynT[:]),
                 reads=[tb_ynT], dma=ch_dump)
        free_scope(scopeW, [tb_wo, tb_ln2st, tb_sqj3] + tb_wo_sl + tb_xb2 + tb_x1b + tb_yn2)
        free_scope(scopeO, [tb_of])

    if phases >= 7:
        P.phase = "ph6_ffnA"
        ch_f1 = [P.chan(f"f1_{i}") for i in range(3)]
        ch_f3 = [P.chan(f"f3_{i}") for i in range(4)]
        ch_s1 = [P.chan(f"s1_{i}") for i in range(2)]
        scopeFA = ExitStack()
        w1b = [sb(f"w1b_{i}", [128, 8192], BF16, scope=scopeFA) for i in range(3)]
        w3b = [sb(f"w3b_{i}", [128, 8192], BF16, scope=scopeFA) for i in range(3)]
        tb_w1b = [mkbuf(f"w1b{i}") for i in range(3)]
        tb_w3b = [mkbuf(f"w3b{i}") for i in range(3)]
        silu32 = [sb(f"silu32_{i}", [128, 1024], F32, scope=scopeFA) for i in range(2)]
        tb_silu = [mkbuf(f"silu{i}") for i in range(2)]
        hidb = [sb(f"hidb_{i}", [128, 1024], BF16, scope=scopeFA) for i in range(2)]
        tb_hidb = [mkbuf(f"hidb{i}") for i in range(2)]
        h1b = [sb(f"h1b_{i}", [128, 1024], BF16, scope=scopeFA) for i in range(2)]
        tb_h1b = [mkbuf(f"h1b{i}") for i in range(2)]
        def emit_h128(a):
            nonlocal wi
            (f0, f1), tbsF = slots[wi % 3]
            wi += 1
            for ns, half in enumerate((f0, f1)):
                P.op("te", lambda e, a=a, ns=ns, half=half: e.matmul(
                    half, H128BF, hidb[a % 2][:, ns * 512:(ns + 1) * 512],
                    start=True, stop=True),
                    reads=[tb_hidb[a % 2], tb_consts], writes=list(tbsF))
            P.op("se", lambda e, a=a, h=f0: e.activation(
                h1b[a % 2][:, 0:512], h, AF.Copy, scale=FWHT_SCALE),
                reads=list(tbsF), writes=[tb_h1b[a % 2]])
            P.op("ve", lambda e, a=a, h=f1: e.tensor_scalar_mul(
                h1b[a % 2][:, 512:1024], h, FWHT_SCALE),
                reads=list(tbsF), writes=[tb_h1b[a % 2]])
            P.op("sp", lambda e, a=a: e.dma_start(
                out=scr1[a * 128:(a + 1) * 128, :], in_=h1b[a % 2][:]),
                reads=[tb_h1b[a % 2]], appends=[tb_scr1], dma=ch_s1[a % 2])

        def load_w13(g):
            P.op("gp", lambda e, g=g: e.dma_start(
                out=w1b[g % 3][:].rearrange("p (a q) -> p a q", q=2048),
                in_=w1S[g * 4:(g + 1) * 4].rearrange("a p q -> p a q")),
                writes=[tb_w1b[g % 3]], dma=ch_f1[g % 3])
            P.op("gp", lambda e, g=g: e.dma_start(
                out=w3b[g % 3][:].rearrange("p (a q) -> p a q", q=2048),
                in_=w3S[g * 4:(g + 1) * 4].rearrange("a p q -> p a q")),
                writes=[tb_w3b[g % 3]], dma=ch_f3[g % 3])

        load_w13(0)
        load_w13(1)
        load_w13(2)
        for a in range(NA):
            if a % 4 == 0:
                g = a // 4
                if 3 <= g + 2 < NA // 4:
                    load_w13(g + 2)
            ai = a % 4
            gg = (a // 4) % 3
            (u10, u11), tbs1 = slots[wi % 3]
            wi += 1
            (u30, u31), tbs3 = slots[wi % 3]
            wi += 1
            for wb, tb_wb, halves, tbsX in ((w1b[gg], tb_w1b[gg], (u10, u11), tbs1),
                                            (w3b[gg], tb_w3b[gg], (u30, u31), tbs3)):
                for dk in range(16):
                    for ns, half in enumerate(halves):
                        P.op("te", lambda e, dk=dk, ai=ai, wb=wb, ns=ns, half=half:
                             e.matmul(half, wb[:, ai * 2048 + dk * 128: ai * 2048 + (dk + 1) * 128],
                                      ynT[:, dk * TC + ns * 512: dk * TC + (ns + 1) * 512],
                                      start=(dk == 0), stop=(dk == 15)),
                             reads=[tb_wb, tb_ynT], writes=list(tbsX))
            for ns in range(2):
                P.op("se", lambda e, a=a, ns=ns, h=(u10, u11)[ns]: e.activation(
                    silu32[a % 2][:, ns * 512:(ns + 1) * 512], h, AF.Silu),
                    reads=list(tbs1), writes=[tb_silu[a % 2]])
                P.op("ve", lambda e, a=a, ns=ns, h=(u30, u31)[ns]: e.tensor_mul(
                    hidb[a % 2][:, ns * 512:(ns + 1) * 512],
                    silu32[a % 2][:, ns * 512:(ns + 1) * 512], h),
                    reads=list(tbs3) + [tb_silu[a % 2]], writes=[tb_hidb[a % 2]])
            if a > 0:
                emit_h128(a - 1)
        emit_h128(NA - 1)
        free_scope(scopeFA, tb_w1b + tb_w3b + tb_silu + tb_hidb + tb_h1b)
        free_scope(scopeY, [tb_ynT])

        P.phase = "ph6_ffnB1"
        ch_b1 = [P.chan(f"b1_{i}") for i in range(3)]
        hfwht = sb("hfwht", [128, NA * TC], BF16, side="right")
        tb_hf = mkbuf("hfwht")
        scopeFB = ExitStack()
        b1in = [sb(f"b1in_{i}", [128, 1024], BF16, scope=scopeFB) for i in range(3)]
        tb_b1in = [mkbuf(f"b1in{i}") for i in range(3)]
        scr1_g = scr1.rearrange("(a x m) t -> x m a t", x=2, m=64)
        for m in range(64):
            for x in range(2):
                q = (2 * m + x) % 3
                P.op("gp", lambda e, m=m, x=x: e.dma_start(
                    out=b1in[m % 3][x * 64:(x + 1) * 64, :],
                    in_=scr1_g[x, m]),
                    reads=[tb_scr1], writes=[tb_b1in[m % 3]], dma=ch_b1[q])
            (h0, h1), tbs = slots[wi % 3]
            wi += 1
            for ns, half in enumerate((h0, h1)):
                P.op("te", lambda e, m=m, ns=ns, half=half: e.matmul(
                    half, h64d_bf[:],
                    b1in[m % 3][:, ns * 512:(ns + 1) * 512],
                    start=True, stop=True),
                    reads=[tb_b1in[m % 3], tb_consts], writes=list(tbs))
                if ns == 0:
                    P.op("se", lambda e, m=m, half=half: e.activation(
                        hfwht[:, m * TC: m * TC + 512], half, AF.Copy),
                        reads=list(tbs), writes=[tb_hf])
                else:
                    P.op("ve", lambda e, m=m, half=half: e.tensor_copy(
                        hfwht[:, m * TC + 512: m * TC + 1024], half),
                        reads=list(tbs), writes=[tb_hf])
        free_scope(scopeFB, tb_b1in)

        P.phase = "ph6_ffnC"
        ch_w2 = [P.chan(f"w2_{i}") for i in range(2)]
        ch_xp = [P.chan(f"xp_{i}") for i in range(3)]
        scopeFC = ExitStack()
        w2b = [sb(f"w2b_{i}", [128, 8192], BF16, scope=scopeFC) for i in range(2)]
        tb_w2bg = [mkbuf(f"w2bg{i}") for i in range(2)]
        tb_w2bs = [mkbuf(f"w2bs{i}") for i in range(2)]
        ch_w2s = [P.chan(f"w2s_{i}") for i in range(2)]
        x1p = [sb(f"x1p_{i}", [128, 1024], F32, scope=scopeFC) for i in range(3)]
        tb_x1p = [mkbuf(f"x1p{i}") for i in range(3)]
        ffnf = [sb(f"ffnf_{i}", [128, 1024], F32, scope=scopeFC) for i in range(2)]
        tb_ffnf = [mkbuf(f"ffnf{i}") for i in range(2)]
        outb = [sb(f"outb_{i}", [128, 1024], F32, scope=scopeFC) for i in range(2)]
        tb_outb = [mkbuf(f"outb{i}") for i in range(2)]
        ch_ob = [P.chan(f"ob{i}") for i in range(2)]
        tb_outd = mkbuf("outd")
        obi = 0
        for dt in range(16):
            P.op("gp", lambda e, dt=dt: e.dma_start(
                out=w2b[dt % 2][:], in_=w2S[dt]),
                writes=[tb_w2bg[dt % 2]], dma=ch_w2[dt % 2])
            P.op("se", lambda e, dt=dt: e.dma_start(
                out=x1p[dt % 3][:].rearrange("p (i j) -> p i j", j=128),
                in_=scrX1.rearrange("(i p) q -> p i q", p=128)[:, :, dt * 128:(dt + 1) * 128]),
                reads=[tb_scrX1], writes=[tb_x1p[dt % 3]], dma=ch_xp[dt % 3])
            (h0, h1), tbs = slots[wi % 3]
            wi += 1
            for a2 in range(NA):
                for ns, half in enumerate((h0, h1)):
                    P.op("te", lambda e, dt=dt, a2=a2, ns=ns, half=half: e.matmul(
                        half, w2b[dt % 2][:, a2 * 128:(a2 + 1) * 128],
                        hfwht[:, a2 * TC + ns * 512: a2 * TC + (ns + 1) * 512],
                        start=(a2 == 0), stop=(a2 == NA - 1)),
                        reads=[tb_w2bg[dt % 2], tb_hf], writes=list(tbs))
            for ns, half in enumerate((h0, h1)):
                P.op("ve", lambda e, dt=dt, ns=ns, half=half: e.tensor_copy(
                    ffnf[dt % 2][:, ns * 512:(ns + 1) * 512], half),
                    reads=list(tbs), writes=[tb_ffnf[dt % 2]])
            ob, tb_ob = outb[dt % 2], tb_outb[dt % 2]
            for i in range(NCH):
                bank, tbk = (psC, tC_) if i % 2 == 0 else (psD, tD)
                w = bank[:, (i // 2) % 2 * 128:((i // 2) % 2 + 1) * 128]
                P.op("te", lambda e, dt=dt, i=i, w=w: e.transpose(
                    w, ffnf[dt % 2][:, i * 128:(i + 1) * 128], IDN32),
                    reads=[tb_ffnf[dt % 2], tb_consts], writes=[tbk])
                P.op("ve", lambda e, dt=dt, i=i, w=w, ob=ob: e.tensor_add(
                    ob[:, i * 128:(i + 1) * 128], w, x1p[dt % 3][:, i * 128:(i + 1) * 128]),
                    reads=[tbk, tb_x1p[dt % 3]], writes=[tb_ob])
            P.op("sp", lambda e, dt=dt, ob=ob: e.dma_start(
                out=out_dram.rearrange("(i p) q -> p i q", p=128)[:, :, dt * 128:(dt + 1) * 128],
                in_=ob[:].rearrange("p (i j) -> p i j", j=128)),
                reads=[tb_ob], appends=[tb_outd], dma=ch_ob[dt % 2])
        scopeFC.close()

    if phases < 7:
        z128 = sb("z128", [128, 128], F32)
        tb_z = mkbuf("z")
        P.op("ve", lambda e: e.memset(z128[:], 0.0), reads=[tb_consts], writes=[tb_z])
        for i in range(NCH):
            for j in range(16):
                P.op("sp", lambda e, i=i, j=j: e.dma_start(
                    out=out_dram[i * 128:(i + 1) * 128, j * 128:(j + 1) * 128],
                    in_=z128[:]),
                    reads=[tb_z], dma=ch_out)

    block = E_(nc.Block())
    P.emit(nc, block, sems_c, sem_cc, mksem)
    ctx.close()
    return nc



def _relayout(inputs):
    f = np.ascontiguousarray
    Wq = inputs["L0_attn_Wq"]; Wk = inputs["L0_attn_Wk"]; Ws = inputs["L0_attn_Ws"]
    Wv = inputs["L0_attn_Wv"]; Wg = inputs["L0_attn_Wg"]; Wo = inputs["L0_attn_Wo"]
    W1 = inputs["L0_ffn_W1"]; W2 = inputs["L0_ffn_W2"]; W3 = inputs["L0_ffn_W3"]

    def colblocks(W):
        A = np.asarray(W, np.float32).reshape(16, 128, 8, 128)
        return f(A.transpose(2, 1, 0, 3).reshape(8, 128, 2048))

    def vhalves(W):
        A = np.asarray(W, np.float32).reshape(16, 128, 2, 1024)
        return f(A.transpose(1, 2, 0, 3).reshape(128, 2, 16384))

    def wo_lay(W):
        A = np.asarray(W, np.float32).reshape(16, 128, 2048)
        return f(A.transpose(1, 0, 2).reshape(128, 32768))

    def w13_lay(W):
        A = np.asarray(W, np.float32).reshape(64, 128, 16, 128)
        return f(A.transpose(0, 3, 2, 1).reshape(64, 128, 2048))

    def w2_lay(W):
        A = np.asarray(W, np.float32).reshape(16, 128, 64, 2, 64)
        return f(A.transpose(0, 3, 2, 4, 1).reshape(16, 128, 8192))

    def hmat(n):
        Hm = np.ones((1, 1), np.float32)
        while Hm.shape[0] < n:
            Hm = np.block([[Hm, Hm], [Hm, -Hm]]).astype(np.float32)
        return Hm

    com = {
        "wqS": colblocks(Wq), "wkS": colblocks(Wk), "wsS": colblocks(Ws),
        "wvS": vhalves(Wv), "wgS": vhalves(Wg), "woS": wo_lay(Wo),
        "w1S": w13_lay(W1), "w3S": w13_lay(W3), "w2S": w2_lay(W2),
        "gnw": f(np.asarray(inputs["L0_attn_g_norm_weight"], np.float32).reshape(128, 1)),
        "tri": np.triu(np.ones((128, 128), np.float32)),
        "idn": np.eye(128, dtype=np.float32),
        "h64d": np.kron(np.eye(2, dtype=np.float32), hmat(64)), "h128c": hmat(128),
        "onesc": np.ones((128, 128), np.float32),
    }
    xfull = np.asarray(inputs["L0_attn_hidden_states"], np.float32).reshape(T, D)
    maps = []
    for i in range(NCORE):
        m = dict(com)
        m["x"] = f(xfull[i * TC:(i + 1) * TC])
        pm = np.zeros((128, 8), np.float32)
        pm[:, :i] = 1.0
        m["pmask"] = pm
        maps.append(m)
    return maps


_CACHE = {}


def kernel(**inputs):
    if "main" not in _CACHE:
        _CACHE["main"] = build_program()
    nc = _CACHE["main"]
    in_maps = _relayout(inputs)
    res = run_bass_kernel_spmd(nc, in_maps, core_ids=list(range(NCORE)))
    out = np.concatenate([r["out"] for r in res.results], axis=0)
    return out.reshape(1, T, D)


# version 17
# speedup vs baseline: 1.0128x; 1.0128x over previous
import numpy as np
import ml_dtypes
import concourse.bass as bass
import concourse.mybir as mybir
from concourse.bass_utils import run_bass_kernel_spmd
from contextlib import ExitStack

F32 = mybir.dt.float32
BF16 = mybir.dt.bfloat16
AF = mybir.ActivationFunctionType
ALU = mybir.AluOpType
AX = mybir.AxisListType.X

NCORE = 8
T = 8192
TC = T // NCORE
D = 2048
H, DK, DV, M = 16, 64, 128, 64
C = 128
NCH = TC // C
HFF = 8192
NA = HFF // 128
EPS = 1e-5
E64 = float(np.exp(-64.0))
FWHT_SCALE = float(np.float32(1.0) / np.float32(np.sqrt(np.float32(8192.0))))

ENGS = ("sp", "gp", "se", "ve", "te")


class TBuf:
    __slots__ = ("w", "r", "multi", "name")

    def __init__(self, name=""):
        self.w = None
        self.r = []
        self.multi = []
        self.name = name


class Chan:
    __slots__ = ("name", "sem", "cnt")

    def __init__(self, name):
        self.name = name
        self.sem = None
        self.cnt = 0


class Op:
    __slots__ = ("eng", "fn", "deps", "chan", "cc", "inc", "gseq", "target", "ph")

    def __init__(self, eng, fn, deps, chan, gseq, cc=False):
        self.eng, self.fn, self.deps, self.chan, self.gseq = eng, fn, deps, chan, gseq
        self.cc = cc
        self.inc = False
        self.target = 0

    @property
    def dma(self):
        return self.chan is not None


class Prog:
    def __init__(self):
        self.ops = []
        self.by_eng = {e: [] for e in ENGS}
        self.chans = []
        self.phase = "init"

    def chan(self, name):
        c = Chan(name)
        self.chans.append(c)
        return c

    def op(self, eng, fn, reads=(), writes=(), appends=(), dma=None, cc=False):
        deps = set()
        for b in reads:
            if b.w is not None:
                deps.add(b.w)
            deps.update(b.multi)
        for b in writes:
            if b.w is not None:
                deps.add(b.w)
            for r in b.r:
                deps.add(r)
            deps.update(b.multi)
        for b in appends:
            if b.w is not None:
                deps.add(b.w)
        o = Op(eng, fn, deps, dma, len(self.ops), cc=cc)
        o.ph = self.phase
        self.ops.append(o)
        self.by_eng[eng].append(o)
        for b in reads:
            b.r.append(o)
        for b in writes:
            b.w = o
            b.r = []
            b.multi = []
        for b in appends:
            b.multi.append(o)
        return o

    @staticmethod
    def _implicit(d, o):
        return d.eng == o.eng == "te" and not d.dma and not o.dma

    def emit(self, nc, block, sems_c, sem_cc, mksem):
        for o in self.ops:
            for d in o.deps:
                assert d.gseq < o.gseq
                if self._implicit(d, o):
                    continue
                d.inc = True
        cnt_c = {e: 0 for e in ENGS}
        for e in ENGS:
            for o in self.by_eng[e]:
                if o.cc:
                    o.target = 1
                elif o.dma:
                    o.chan.cnt += 16
                    o.target = o.chan.cnt
                else:
                    if o.inc:
                        cnt_c[e] += 1
                    o.target = cnt_c[e]
        for c in self.chans:
            if c.cnt > 0:
                c.sem = mksem("ch_" + c.name)
        self.final_counts = cnt_c
        prog = self

        def emit_engine(eng_obj, e):
            waited = {}
            cur_ph = None
            sid = None
            for o in prog.by_eng[e]:
                if o.ph != cur_ph:
                    if sid is not None:
                        nc.leave_named_scope(f"{cur_ph}_{e}", sid, False)
                    cur_ph = o.ph
                    sid, _ = nc.enter_named_scope(f"{cur_ph}_{e}", False)
                for d in sorted(o.deps, key=lambda d: d.gseq):
                    if Prog._implicit(d, o):
                        continue
                    if d.cc:
                        sem, key = sem_cc, ("cc",)
                    elif d.dma:
                        sem, key = d.chan.sem, ("ch", id(d.chan))
                    else:
                        sem, key = sems_c[d.eng], (d.eng,)
                    if waited.get(key, -1) >= d.target:
                        continue
                    eng_obj.wait_ge(sem, d.target)
                    waited[key] = d.target
                inst = o.fn(eng_obj)
                if o.cc:
                    inst.then_inc(sem_cc)
                elif o.dma:
                    inst.then_inc(o.chan.sem, 16)
                elif o.inc:
                    assert inst is not None
                    inst.then_inc(sems_c[e], 1)
            if sid is not None:
                nc.leave_named_scope(f"{cur_ph}_{e}", sid, False)
            if e == "sp":
                for c in prog.chans:
                    if c.cnt > 0:
                        eng_obj.wait_ge(c.sem, c.cnt)

        @block.sync
        def _(sync):
            emit_engine(sync, "sp")

        @block.gpsimd
        def _(gp):
            emit_engine(gp, "gp")

        @block.scalar
        def _(se):
            emit_engine(se, "se")

        @block.vector
        def _(ve):
            emit_engine(ve, "ve")

        @block.tensor
        def _(te):
            emit_engine(te, "te")


def build_program(dump=(), phases=99):
    nc = bass.Bass("TRN2", target_bir_lowering=False)
    dump = set(dump)

    x_in = nc.dram_tensor("x", [TC, D], F32, kind="ExternalInput").ap()
    wqS = nc.dram_tensor("wqS", [8, 128, 2048], F32, kind="ExternalInput").ap()
    wkS = nc.dram_tensor("wkS", [8, 128, 2048], F32, kind="ExternalInput").ap()
    wsS = nc.dram_tensor("wsS", [8, 128, 2048], F32, kind="ExternalInput").ap()
    wvS = nc.dram_tensor("wvS", [128, 2, 16384], F32, kind="ExternalInput").ap()
    wgS = nc.dram_tensor("wgS", [128, 2, 16384], F32, kind="ExternalInput").ap()
    woS = nc.dram_tensor("woS", [128, 32768], F32, kind="ExternalInput").ap()
    w1S = nc.dram_tensor("w1S", [NA, 128, 2048], F32, kind="ExternalInput").ap()
    w3S = nc.dram_tensor("w3S", [NA, 128, 2048], F32, kind="ExternalInput").ap()
    w2S = nc.dram_tensor("w2S", [16, 128, 8192], F32, kind="ExternalInput").ap()
    gnw_in = nc.dram_tensor("gnw", [128, 1], F32, kind="ExternalInput").ap()
    tri_in = nc.dram_tensor("tri", [128, 128], F32, kind="ExternalInput").ap()
    idn_in = nc.dram_tensor("idn", [128, 128], F32, kind="ExternalInput").ap()
    h64d_in = nc.dram_tensor("h64d", [128, 128], F32, kind="ExternalInput").ap()
    h128_in = nc.dram_tensor("h128c", [128, 128], F32, kind="ExternalInput").ap()
    ones_in = nc.dram_tensor("onesc", [128, 128], F32, kind="ExternalInput").ap()
    pmask_in = nc.dram_tensor("pmask", [128, 8], F32, kind="ExternalInput").ap()
    out_dram = nc.dram_tensor("out", [TC, D], F32, kind="ExternalOutput").ap()

    cc_in = nc.dram_tensor("cc_in", [128, 2560], BF16, kind="Internal")
    cc_out = nc.dram_tensor("cc_out", [1024, 2560], BF16, kind="Internal")
    scr1 = nc.dram_tensor("scr1", [HFF, TC], BF16, kind="Internal").ap()
    scrX1 = nc.dram_tensor("scrX1", [TC, 2048], F32, kind="Internal").ap()
    woB = nc.dram_tensor("woB", [128, 32768], BF16, kind="Internal").ap()
    w2B = nc.dram_tensor("w2B", [16, 128, 8192], BF16, kind="Internal").ap()

    dump_aps = {}

    def dumpt(name, shape, dtype=F32):
        if name in dump:
            dump_aps[name] = nc.dram_tensor("dmp_" + name, shape, dtype,
                                            kind="ExternalOutput").ap()
            return dump_aps[name]
        return None

    P = Prog()
    tb_scr1 = TBuf("scr1d")
    tb_scrX1 = TBuf("scrX1d")
    ctx = ExitStack()
    E_ = ctx.enter_context

    cur_barrier = []

    def mkbuf(name=""):
        tb = TBuf(name)
        tb.r = list(cur_barrier)
        return tb

    def free_scope(scope, tbufs):
        ops = []
        for tb in tbufs:
            if tb.w is not None:
                ops.append(tb.w)
            ops.extend(tb.r)
        cur_barrier.extend(ops)
        scope.close()

    def sb(name, shape, dtype, scope=None, side=None):
        kw = {"side": side} if side else {}
        return (scope or ctx).enter_context(nc.sbuf_tensor(name, shape, dtype, **kw))

    psA = E_(nc.psum_tensor([128, 512], F32))
    psB = E_(nc.psum_tensor([128, 512], F32))
    psC = E_(nc.psum_tensor([128, 512], F32))
    psD = E_(nc.psum_tensor([128, 512], F32))
    psE = E_(nc.psum_tensor([128, 1024], F32))
    psF = E_(nc.psum_tensor([128, 1024], BF16))
    psG = E_(nc.psum_tensor([128, 1024], BF16))
    tA, tB, tC_, tD = TBuf("psA"), TBuf("psB"), TBuf("psC"), TBuf("psD")
    tE, tF, tG = TBuf("psE"), TBuf("psF"), TBuf("psG")

    sems_c = {e: E_(nc.semaphore(f"s_c_{e}")) for e in ENGS}
    sem_cc = E_(nc.semaphore("s_cc"))
    mksem = lambda n: E_(nc.semaphore(n))
    ch_const = P.chan("const")
    ch_x = [P.chan(f"x{i}") for i in range(2)]
    ch_w = [P.chan(f"w{i}") for i in range(3)]
    ch_wbig = P.chan("wbig")
    ch_dump = P.chan("dump")
    ch_cc = P.chan("ccin")
    ch_rk = [P.chan(f"rk{i}") for i in range(4)]
    ch_out = P.chan("outp")

    consts32 = sb("consts32", [128, 512], F32)
    h64d_32 = sb("h64d_32", [128, 128], F32)
    constsbf = sb("constsbf", [128, 512], BF16)
    h64d_bf = sb("h64d_bf", [128, 128], BF16)
    gnw_sb = sb("gnw_sb", [128, 1], F32)
    pmask_sb = sb("pmask_sb", [128, 8], F32)
    cbias = sb("cbias", [128, 2], F32)
    nc.const_aps.register(F32, -32.0, cbias[:, 0:1])
    nc.const_aps.register(F32, 1e-5, cbias[:, 1:2])

    TRI32 = consts32[:, 0:128]
    IDN32 = consts32[:, 128:256]
    TRIBF = constsbf[:, 0:128]
    IDNBF = constsbf[:, 128:256]
    H128BF = constsbf[:, 256:384]
    ONEROWBF = constsbf[0:1, 384:512]
    ONECOLBF = constsbf[:, 384:385]

    scopeB = ExitStack()
    xnT = sb("xnT", [128, 16 * TC], BF16, scope=scopeB, side="right")
    scopeWS = ExitStack()
    wstr = [sb(f"wstr{i}", [128, 2048], BF16, scope=scopeWS, side="right") for i in range(2)]
    tb_wstr = [mkbuf(f"wstr{i}") for i in range(2)]
    scopeS1 = ExitStack()
    p32f = sb("p32f", [128, 8 * TC], BF16, scope=scopeS1)
    kf = sb("kf", [128, 8 * TC], BF16, scope=scopeS1)
    qf = sb("qf", [128, 8 * TC], BF16, scope=scopeS1)
    vt = sb("vt", [128, 8 * 2048], BF16, scope=scopeS1)
    sg = sb("sg", [128, 8 * 2048], BF16, scope=scopeS1)
    tb_vt, tb_sg = mkbuf("vt"), mkbuf("sg")

    tb_consts = TBuf("consts")
    tb_xnT = TBuf("xnT")
    tb_xnTh = [TBuf("xnT0"), TBuf("xnT1")]
    tb_p32f = TBuf("p32f")
    tb_kf = TBuf("kf")
    tb_qf = TBuf("qf")

    P.phase = "ph0"
    P.op("sp", lambda e: e.dma_start(out=consts32[:, 0:128], in_=tri_in), writes=[tb_consts], dma=ch_const)
    P.op("sp", lambda e: e.dma_start(out=consts32[:, 128:256], in_=idn_in), writes=[tb_consts], dma=ch_const)
    P.op("sp", lambda e: e.dma_start(out=consts32[:, 256:384], in_=h128_in), writes=[tb_consts], dma=ch_const)
    P.op("sp", lambda e: e.dma_start(out=consts32[:, 384:512], in_=ones_in), writes=[tb_consts], dma=ch_const)
    P.op("sp", lambda e: e.dma_start(out=h64d_32[:], in_=h64d_in), writes=[tb_consts], dma=ch_const)
    P.op("sp", lambda e: e.dma_start(out=gnw_sb[:], in_=gnw_in), writes=[tb_consts], dma=ch_const)
    P.op("sp", lambda e: e.dma_start(out=pmask_sb[:], in_=pmask_in), writes=[tb_consts], dma=ch_const)
    P.op("ve", lambda e: e.tensor_copy(constsbf[:], consts32[:]), reads=[tb_consts], writes=[tb_consts])
    P.op("ve", lambda e: e.tensor_copy(h64d_bf[:], h64d_32[:]), reads=[tb_consts], writes=[tb_consts])
    P.op("ve", lambda e: e.memset(cbias[:, 0:1], -32.0), writes=[tb_consts])
    P.op("ve", lambda e: e.memset(cbias[:, 1:2], 1e-5), writes=[tb_consts])

    P.phase = "ph1_ln1"
    scopeA = ExitStack()
    xbuf = [sb(f"xbuf{i}", [128, D], F32, scope=scopeA) for i in range(2)]
    tb_x = [mkbuf(f"x{i}") for i in range(2)]
    lnstat = [sb(f"lnstat{i}", [128, 8], F32, scope=scopeA) for i in range(2)]
    tb_lnstat = [mkbuf(f"lnstat{i}") for i in range(2)]
    xn_tm = [sb(f"xn_tm{i}", [128, D], BF16, scope=scopeA) for i in range(2)]
    tb_xn_tm = [mkbuf(f"xn_tm{i}") for i in range(2)]
    sqjunk = sb("sqjunk", [128, D], BF16, scope=scopeA)
    tb_sqjunk = mkbuf("sqjunk")

    def emit_ln(xb, tb_xb, xnb, tb_xnb, lnstat, tb_lnstat, sqjunk, tb_sqjunk):
        P.op("ve", lambda e: e.reduce_sum(lnstat[:, 0:1], xb[:], axis=AX),
             reads=[tb_xb], writes=[tb_lnstat])
        P.op("se", lambda e: e.activation(sqjunk[:], xb[:], AF.Square,
                                          accum_out=lnstat[:, 1:2]),
             reads=[tb_xb], writes=[tb_lnstat, tb_sqjunk])
        P.op("ve", lambda e: e.tensor_scalar_mul(lnstat[:, 2:3], lnstat[:, 0:1], 1.0 / D),
             reads=[tb_lnstat], writes=[tb_lnstat])
        P.op("ve", lambda e: e.tensor_mul(lnstat[:, 3:4], lnstat[:, 2:3], lnstat[:, 2:3]),
             reads=[tb_lnstat], writes=[tb_lnstat])
        P.op("ve", lambda e: e.tensor_scalar(lnstat[:, 4:5], lnstat[:, 1:2], 1.0 / D,
                                             None, op0=ALU.mult),
             reads=[tb_lnstat], writes=[tb_lnstat])
        P.op("ve", lambda e: e.tensor_sub(lnstat[:, 4:5], lnstat[:, 4:5], lnstat[:, 3:4]),
             reads=[tb_lnstat], writes=[tb_lnstat])
        P.op("se", lambda e: e.activation(lnstat[:, 5:6], lnstat[:, 4:5], AF.Ln, bias=1e-5),
             reads=[tb_lnstat], writes=[tb_lnstat])
        P.op("se", lambda e: e.activation(lnstat[:, 6:7], lnstat[:, 5:6], AF.Exp, scale=-0.5),
             reads=[tb_lnstat], writes=[tb_lnstat])
        P.op("ve", lambda e: e.tensor_mul(lnstat[:, 7:8], lnstat[:, 2:3], lnstat[:, 6:7]),
             reads=[tb_lnstat], writes=[tb_lnstat])
        P.op("ve", lambda e: e.tensor_scalar_mul(lnstat[:, 7:8], lnstat[:, 7:8], -1.0),
             reads=[tb_lnstat], writes=[tb_lnstat])
        P.op("ve", lambda e: e.tensor_scalar(xnb[:], xb[:], lnstat[:, 6:7],
                                             lnstat[:, 7:8], op0=ALU.mult, op1=ALU.add),
             reads=[tb_xb, tb_lnstat], writes=[tb_xnb])

    for i in range(NCH):
        xb, tb_xb = xbuf[i % 2], tb_x[i % 2]
        xnb, tb_xnb = xn_tm[i % 2], tb_xn_tm[i % 2]
        P.op("sp", lambda e, i=i, xb=xb: e.dma_start(
            out=xb[:], in_=x_in[i * 128:(i + 1) * 128, :]),
            writes=[tb_xb], dma=ch_x[i % 2])
        emit_ln(xb, tb_xb, xnb, tb_xnb, lnstat[i % 2], tb_lnstat[i % 2], sqjunk, tb_sqjunk)
        for dt in range(16):
            bank, tbk = (psF, tF) if dt % 2 == 0 else (psG, tG)
            w = bank[:, (dt // 2) % 4 * 128:((dt // 2) % 4 + 1) * 128]
            P.op("te", lambda e, dt=dt, xnb=xnb, w=w: e.transpose(
                w, xnb[:, dt * 128:(dt + 1) * 128], IDNBF),
                reads=[tb_xnb, tb_consts], writes=[tbk])
            P.op("ve", lambda e, dt=dt, i=i, w=w: e.tensor_copy(
                xnT[:, dt * TC + i * 128: dt * TC + (i + 1) * 128], w),
                reads=[tbk], writes=[tb_xnTh[i // 4]], appends=[tb_xnT])

    P.phase = "ph2_skq"
    slots = [((psE[:, 0:512], psE[:, 512:1024]), (tE,)),
             ((psA[:, 0:512], psB[:, 0:512]), (tA, tB)),
             ((psC[:, 0:512], psD[:, 0:512]), (tC_, tD))]

    wi = 0
    for pname, wsrc, dst, tb_dst in [("s", wsS, p32f, tb_p32f),
                                     ("k", wkS, kf, tb_kf)]:
        for ct in range(8):
            wb, tb_wb = wstr[wi % 2], tb_wstr[wi % 2]
            (h0, h1), tbs = slots[wi % 3]
            wi += 1
            P.op("gp", lambda e, wsrc=wsrc, ct=ct, wb=wb: e.dma_start(
                out=wb[:], in_=wsrc[ct]), writes=[tb_wb], dma=ch_w[(wi - 1) % 3])
            for dk in range(16):
                for ns, half in enumerate((h0, h1)):
                    P.op("te", lambda e, dk=dk, ns=ns, wb=wb, half=half:
                         e.matmul(half,
                                  wb[:, dk * 128:(dk + 1) * 128],
                                  xnT[:, dk * TC + ns * 512: dk * TC + (ns + 1) * 512],
                                  start=(dk == 0), stop=(dk == 15)),
                         reads=[tb_wb, tb_xnTh[ns]], writes=list(tbs))
            o = ct * TC
            if pname == "s":
                for ns, half in enumerate((h0, h1)):
                    P.op("se", lambda e, o=o, ns=ns, half=half: e.activation(
                        p32f[:, o + ns * 512: o + (ns + 1) * 512], half, AF.Exp, bias=-32.0),
                        reads=list(tbs), writes=[tb_p32f])
                P.op("ve", lambda e, o=o: e.tensor_scalar(
                    p32f[:, o:o + TC], p32f[:, o:o + TC], E64, 1.0,
                    op0=ALU.max, op1=ALU.min),
                    reads=[tb_p32f], writes=[tb_p32f])
            elif pname == "k":
                for ns, half in enumerate((h0, h1)):
                    P.op("ve", lambda e, o=o, ns=ns, half=half: e.tensor_copy(
                        kf[:, o + ns * 512: o + (ns + 1) * 512], half),
                        reads=list(tbs), writes=[tb_kf])
            else:
                for ns, half in enumerate((h0, h1)):
                    P.op("se", lambda e, o=o, ns=ns, half=half: e.activation(
                        qf[:, o + ns * 512: o + (ns + 1) * 512], half, AF.Copy, scale=0.125),
                        reads=list(tbs), writes=[tb_qf])

    for nm, src, tb_src in [("xnT", xnT, tb_xnT), ("p32f", p32f, tb_p32f),
                            ("kf", kf, tb_kf), ("qf", qf, tb_qf)]:
        ap = dumpt(nm, [128, 8192 if nm != "xnT" else 16384], BF16)
        if ap is not None:
            P.op("sp", lambda e, ap=ap, src=src: e.dma_start(out=ap, in_=src[:]),
                 reads=[tb_src], dma=ch_dump)

    free_scope(scopeA, tb_x + tb_lnstat + [tb_sqjunk] + tb_xn_tm + tb_wstr)

    P.phase = "ph2b_vg"
    if phases >= 3:
        scopeC = ExitStack()
        wbig = sb("wbig", [128, 16384], BF16, scope=scopeC, side="right")
        tb_wbig = mkbuf("sgw")
        for pname, wsrc, dst, tb_dst in [("v", wvS, vt, tb_vt)]:
            for vh in range(2):
                P.op("gp", lambda e, wsrc=wsrc, vh=vh: e.dma_start(
                    out=wbig[:], in_=wsrc[:, vh, :]), writes=[tb_wbig], dma=ch_wbig)
                for i in range(NCH):
                    (h0, h1), tbs = slots[wi % 3]
                    wi += 1
                    for dk in range(16):
                        for ns, half in enumerate((h0, h1)):
                            P.op("te", lambda e, dk=dk, ns=ns, i=i, half=half:
                                 e.matmul(half,
                                          xnT[:, dk * TC + i * 128:dk * TC + (i + 1) * 128],
                                          wbig[:, dk * 1024 + ns * 512: dk * 1024 + (ns + 1) * 512],
                                          start=(dk == 0), stop=(dk == 15)),
                                 reads=[tb_wbig, tb_xnTh[i // 4]], writes=list(tbs))
                    o = i * 2048 + vh * 1024
                    for ns, half in enumerate((h0, h1)):
                        if pname == "v":
                            P.op("ve", lambda e, o=o, ns=ns, half=half: e.tensor_copy(
                                vt[:, o + ns * 512: o + (ns + 1) * 512], half),
                                reads=list(tbs), writes=[tb_vt])
                        else:
                            P.op("se", lambda e, o=o, ns=ns, half=half: e.activation(
                                sg[:, o + ns * 512: o + (ns + 1) * 512], half, AF.Silu),
                                reads=list(tbs), writes=[tb_sg])
        for nm, src, tb_src in [("vt", vt, tb_vt), ("sg", sg, tb_sg)]:
            ap = dumpt(nm, [128, 16384], BF16)
            if ap is not None:
                P.op("sp", lambda e, ap=ap, src=src: e.dma_start(out=ap, in_=src[:]),
                     reads=[tb_src], dma=ch_dump)

    P.phase = "ph3_stage1"
    if phases >= 4:
        scopeAcc = ExitStack()
        acc = sb("acc", [128, 2560], F32, scope=scopeAcc)
        tb_acc = mkbuf("acc")
        scopeS2 = ExitStack()
        p32t_sb = [sb(f"p32t{i}", [128, 128], BF16, scope=scopeS2) for i in range(3)]
        kt_sb = [sb(f"kt{i}", [128, 128], BF16, scope=scopeS2) for i in range(3)]
        tb_p32t = [mkbuf(f"p32t{i}") for i in range(3)]
        tb_kt = [mkbuf(f"kt{i}") for i in range(3)]
        A1_sb = acc[0:1, 1536:2560]
        tb_A1 = tb_acc
        ccstage, tb_ccstage = acc, tb_acc
        tb_ccin = mkbuf("ccin")
        tb_ccout = mkbuf("ccout")
        scopeR = ExitStack()
        rkbuf = [sb(f"rk{i}", [128, 640], F32, scope=scopeR) for i in range(2)]
        tb_rk = [mkbuf(f"rk{i}") for i in range(2)]

        P.op("ve", lambda e: e.memset(acc[:, 1536:2560], 0.0), writes=[tb_acc])

        def emit_tr_pair(c, hp):
            bank, tbk = (psF, tF) if hp % 2 == 0 else (psG, tG)
            pw = bank[:, 0:128]
            kw = bank[:, 128:256]
            P.op("te", lambda e, c=c, hp=hp, pw=pw: e.transpose(
                pw, p32f[:, hp * TC + c * 128: hp * TC + (c + 1) * 128], IDNBF),
                reads=[tb_p32f, tb_consts], writes=[tbk])
            P.op("te", lambda e, c=c, hp=hp, kw=kw: e.transpose(
                kw, kf[:, hp * TC + c * 128: hp * TC + (c + 1) * 128], IDNBF),
                reads=[tb_kf, tb_consts], writes=[tbk])
            P.op("se", lambda e, hp=hp, pw=pw: e.activation(p32t_sb[hp % 3][:], pw, AF.Copy),
                 reads=[tbk], writes=[tb_p32t[hp % 3]])
            P.op("se", lambda e, hp=hp, kw=kw: e.activation(kt_sb[hp % 3][:], kw, AF.Copy),
                 reads=[tbk], writes=[tb_kt[hp % 3]])
            return bank, tbk

        for c in range(NCH):
            for hp in range(8):
                emit_tr_pair(c, hp)
                for hh in range(2):
                    h = 2 * hp + hh
                    P.op("te", lambda e, c=c, hp=hp, hh=hh: e.matmul(
                        psB[hh * 64:(hh + 1) * 64, hp * 64:(hp + 1) * 64],
                        kt_sb[hp % 3][:, hh * 64:(hh + 1) * 64],
                        p32t_sb[hp % 3][:, hh * 64:(hh + 1) * 64],
                        start=True, stop=True,
                        tile_position=(0, hh * 64)),
                        reads=[tb_kt[hp % 3], tb_p32t[hp % 3]], writes=[tB])
                    P.op("te", lambda e, c=c, hp=hp, hh=hh, h=h: e.matmul(
                        psE[hh * 64:(hh + 1) * 64, hp * 128:(hp + 1) * 128],
                        p32t_sb[hp % 3][:, hh * 64:(hh + 1) * 64],
                        vt[:, c * 2048 + h * 128: c * 2048 + (h + 1) * 128],
                        start=True, stop=True,
                        tile_position=(0, hh * 64)),
                        reads=[tb_p32t[hp % 3], tb_vt], writes=[tE])
                csb, tcs = (psA, tA) if hp % 2 == 0 else (psC, tC_)
                P.op("te", lambda e, hp=hp, csb=csb: e.matmul(
                    csb[0:1, 0:128], ONECOLBF, p32t_sb[hp % 3][:], start=True, stop=True),
                    reads=[tb_p32t[hp % 3], tb_consts], writes=[tcs])
                if c == 0:
                    P.op("ve", lambda e, hp=hp, csb=csb: e.tensor_copy(
                        acc[0:1, 1536 + hp * 128:1536 + (hp + 1) * 128], csb[0:1, 0:128]),
                        reads=[tcs], writes=[tb_A1])
                else:
                    P.op("ve", lambda e, hp=hp, csb=csb: e.tensor_add(
                        acc[0:1, 1536 + hp * 128:1536 + (hp + 1) * 128],
                        acc[0:1, 1536 + hp * 128:1536 + (hp + 1) * 128], csb[0:1, 0:128]),
                        reads=[tcs, tb_A1], writes=[tb_A1])
            if c == 0:
                P.op("ve", lambda e: e.tensor_copy(ccstage[:, 0:512], psB[:]),
                     reads=[tB], writes=[tb_ccstage])
                P.op("ve", lambda e: e.tensor_copy(ccstage[:, 512:1536], psE[:]),
                     reads=[tB, tE], writes=[tb_ccstage])
            else:
                opB = P.op("ve", lambda e: e.tensor_add(ccstage[:, 0:512], ccstage[:, 0:512], psB[:]),
                     reads=[tB, tb_ccstage], writes=[tb_ccstage])
                opE = P.op("ve", lambda e: e.tensor_add(ccstage[:, 512:1536], ccstage[:, 512:1536], psE[:]),
                     reads=[tE, tb_ccstage], writes=[tb_ccstage])
                if c == NCH - 1:
                    last_psB_reader, last_psE_reader = opB, opE

        P.phase = "ph3b_cc"
        P.op("gp", lambda e: e.dma_start(out=cc_in[:, :], in_=ccstage[:]),
             reads=[tb_ccstage], writes=[tb_ccin], dma=ch_cc)
        P.op("gp", lambda e: e.collective_compute(
            "AllGather", ALU.bypass, ins=[cc_in.ap().opt()], outs=[cc_out.ap().opt()],
            replica_groups=[list(range(NCORE))]),
            reads=[tb_ccin], writes=[tb_ccout], cc=True)

        P.phase = "ph3c_gq"
        for pname, wsrc, dst, tb_dst in [("g", wgS, sg, tb_sg)]:
            for vh in range(2):
                P.op("gp", lambda e, wsrc=wsrc, vh=vh: e.dma_start(
                    out=wbig[:], in_=wsrc[:, vh, :]), writes=[tb_wbig], dma=ch_wbig)
                for i in range(NCH):
                    (h0, h1), tbs = slots[wi % 3]
                    wi += 1
                    for dk in range(16):
                        for ns, half in enumerate((h0, h1)):
                            P.op("te", lambda e, dk=dk, ns=ns, i=i, half=half:
                                 e.matmul(half,
                                          xnT[:, dk * TC + i * 128:dk * TC + (i + 1) * 128],
                                          wbig[:, dk * 1024 + ns * 512: dk * 1024 + (ns + 1) * 512],
                                          start=(dk == 0), stop=(dk == 15)),
                                 reads=[tb_wbig, tb_xnTh[i // 4]], writes=list(tbs))
                    o = i * 2048 + vh * 1024
                    for ns, half in enumerate((h0, h1)):
                        P.op("se", lambda e, o=o, ns=ns, half=half: e.activation(
                            sg[:, o + ns * 512: o + (ns + 1) * 512], half, AF.Silu),
                            reads=list(tbs), writes=[tb_sg])
        free_scope(scopeC, [tb_wbig])
        for pname, wsrc, dst, tb_dst in [("q", wqS, qf, tb_qf)]:
            for ct in range(8):
                wb, tb_wb = wstr[wi % 2], tb_wstr[wi % 2]
                (h0, h1), tbs = slots[wi % 3]
                wi += 1
                P.op("gp", lambda e, wsrc=wsrc, ct=ct, wb=wb: e.dma_start(
                    out=wb[:], in_=wsrc[ct]), writes=[tb_wb], dma=ch_w[(wi - 1) % 3])
                for dk in range(16):
                    for ns, half in enumerate((h0, h1)):
                        P.op("te", lambda e, dk=dk, ns=ns, wb=wb, half=half:
                             e.matmul(half,
                                      wb[:, dk * 128:(dk + 1) * 128],
                                      xnT[:, dk * TC + ns * 512: dk * TC + (ns + 1) * 512],
                                      start=(dk == 0), stop=(dk == 15)),
                             reads=[tb_wb, tb_xnT], writes=list(tbs))
                o = ct * TC
                for ns, half in enumerate((h0, h1)):
                    P.op("se", lambda e, o=o, ns=ns, half=half: e.activation(
                        qf[:, o + ns * 512: o + (ns + 1) * 512], half, AF.Copy, scale=0.125),
                        reads=list(tbs), writes=[tb_qf])
        free_scope(scopeWS, tb_wstr)
        P.phase = "ph3d_msum"
        P.op("ve", lambda e: e.memset(acc[:], 0.0), reads=[tb_ccin], writes=[tb_acc])
        for j in range(NCORE):
            for hf in range(4):
                rb, tb_rb = rkbuf[hf % 2], tb_rk[hf % 2]
                P.op("gp", lambda e, j=j, hf=hf, rb=rb: e.dma_start(
                    out=rb[:], in_=cc_out[j * 128:(j + 1) * 128,
                                          hf * 640:(hf + 1) * 640]),
                    reads=[tb_ccout], writes=[tb_rb], dma=ch_rk[hf % 2])
                P.op("ve", lambda e, j=j, hf=hf, rb=rb: e.scalar_tensor_tensor(
                    acc[:, hf * 640:(hf + 1) * 640], rb[:], pmask_sb[:, j:j + 1],
                    acc[:, hf * 640:(hf + 1) * 640],
                    op0=ALU.mult, op1=ALU.add),
                    reads=[tb_rb, tb_consts, tb_acc], writes=[tb_acc])
        ap = dumpt("acc", [128, 2560], F32)
        if ap is not None:
            P.op("sp", lambda e, ap=ap: e.dma_start(out=ap, in_=acc[:]),
                 reads=[tb_acc], dma=ch_dump)
        free_scope(scopeR, [tb_ccstage] + tb_rk)
        free_scope(scopeB, [tb_xnT] + tb_xnTh)
        A_sbbf = sb("A_sbbf", [1, 1024], BF16, scope=scopeS2)
        tb_Abf = mkbuf("A_sbbf")
        hk_sb = sb("hk_sb", [128, 512], BF16, scope=scopeS2)
        hv_sb = sb("hv_sb", [128, 1024], BF16, scope=scopeS2)
        tb_hk, tb_hv = mkbuf("hk_sb"), mkbuf("hv_sb")
        hk32_sb = acc[:, 0:512]
        hv32_sb = acc[:, 512:1536]
        A_sb = acc[0:1, 1536:2560]
        tb_hk32 = tb_hv32 = tb_As = tb_acc

    P.phase = "ph4_scan"
    if phases >= 5:
        ch_pc = P.chan("preconv")
        tb_woB = mkbuf("woB")
        tb_w2B = mkbuf("w2B")
        for q in range(4):
            P.op("gp", lambda e, q=q: e.dma_start(
                out=woB[:, q * 8192:(q + 1) * 8192],
                in_=woS[:, q * 8192:(q + 1) * 8192]),
                writes=[tb_woB], dma=ch_pc)
        scopeY = ExitStack()
        ynT = sb("ynT", [128, 16 * TC], BF16, scope=scopeY, side="right")
        tb_ynT = mkbuf("ynT")
        P.op("ve", lambda e: e.tensor_copy(A_sbbf[:], A_sb),
             reads=[tb_As], writes=[tb_Abf])
        scopeO = ExitStack()
        of_sb = sb("of_sb", [128, 16 * TC], BF16, scope=scopeO, side="right")
        tb_of = mkbuf("of")
        u_pair = [sb(f"u_pair{i}", [128, 128], BF16, scope=scopeS2) for i in range(3)]
        tb_up = [mkbuf(f"up{i}") for i in range(3)]
        uf_sb = [sb(f"uf{i}", [128, 128], BF16, scope=scopeS2) for i in range(3)]
        tb_uf = [mkbuf(f"uf{i}") for i in range(3)]
        s1t_sb = [sb(f"s1t{i}", [128, 128], BF16, scope=scopeS2) for i in range(4)]
        tb_s1t = [mkbuf(f"s1t{i}") for i in range(4)]
        wov_sb = [sb(f"wov{i}", [128, 128], BF16, scope=scopeS2) for i in range(4)]
        tb_wov = [mkbuf(f"wov{i}") for i in range(4)]
        rc_sb = [sb(f"rc{i}", [128, 64], F32, scope=scopeS2) for i in range(4)]
        ok_sb = [sb(f"okb{i}", [128, 64], F32, scope=scopeS2) for i in range(4)]
        e_sb = [sb(f"eb{i}", [128, 64], F32, scope=scopeS2) for i in range(4)]
        st_sb = [sb(f"stb{i}", [128, 4], F32, scope=scopeS2) for i in range(4)]
        st2_sb = [sb(f"st2b{i}", [128, 4], F32, scope=scopeS2) for i in range(4)]
        og_sb = [sb(f"og{i}", [128, 128], BF16, scope=scopeS2) for i in range(4)]
        sqj2 = [sb("sqj2_0", [128, 128], BF16, scope=scopeS2)] * 4
        tb_rc = [mkbuf(f"rc{i}") for i in range(4)]
        tb_ok = [mkbuf(f"ok{i}") for i in range(4)]
        tb_e = [mkbuf(f"e{i}") for i in range(4)]
        tb_st = [mkbuf(f"st{i}") for i in range(4)]
        tb_st2 = [mkbuf(f"st2{i}") for i in range(4)]
        tb_og = [mkbuf(f"og{i}") for i in range(4)]
        tb_sqj2 = [mkbuf("sqj0")] * 4

        pools = [(psA, tA), (psC, tC_), (psD, tD)]
        tB_hp, tE_hp = [], []
        for hp in range(8):
            t1 = TBuf(f"psBs{hp}")
            t1.r = [last_psB_reader]
            t2 = TBuf(f"psEs{hp}")
            t2.r = [last_psE_reader]
            tB_hp.append(t1)
            tE_hp.append(t2)
        for c in range(NCH):
            P.op("ve", lambda e: e.tensor_copy(hk_sb[:], hk32_sb),
                 reads=[tb_hk32], writes=[tb_hk])
            P.op("ve", lambda e: e.tensor_copy(hv_sb[:], hv32_sb),
                 reads=[tb_hv32], writes=[tb_hv])
            if c > 0:
                P.op("ve", lambda e: e.tensor_copy(A_sbbf[:], A_sb),
                     reads=[tb_As], writes=[tb_Abf])
            for hp in range(8):
                bank, tbk = emit_tr_pair(c, hp)
                UFW = bank[:, 256:384]
                OTW = [bank[:, 384:512], bank[:, 512:640]]
                hs = []
                for hh in range(2):
                    g = hp * 2 + hh
                    pool, tpool = pools[g % 3]
                    hs.append(dict(h=g, hh=hh, g=g, pool=pool, tp=tpool,
                                   sl=slice(hh * 64, (hh + 1) * 64),
                                   toff=hp * TC + c * 128))
                pt, tb_pt = p32t_sb[hp % 3], tb_p32t[hp % 3]
                kt, tb_ktb = kt_sb[hp % 3], tb_kt[hp % 3]
                for s in hs:
                    P.op("te", lambda e, s=s, hp=hp, pt=pt, kt=kt: e.matmul(
                        psB[s["sl"], hp * 64:(hp + 1) * 64],
                        kt[:, s["sl"]], pt[:, s["sl"]],
                        start=True, stop=True, tile_position=(0, s["hh"] * 64)),
                        reads=[tb_ktb, tb_pt], writes=[tB_hp[hp]])
                    P.op("te", lambda e, s=s, hp=hp, c=c, pt=pt: e.matmul(
                        psE[s["sl"], hp * 128:(hp + 1) * 128],
                        pt[:, s["sl"]],
                        vt[:, c * 2048 + s["h"] * 128: c * 2048 + (s["h"] + 1) * 128],
                        start=True, stop=True, tile_position=(0, s["hh"] * 64)),
                        reads=[tb_pt, tb_vt], writes=[tE_hp[hp]])
                P.op("ve", lambda e, hp=hp: e.tensor_add(
                    hk32_sb[:, hp * 64:(hp + 1) * 64],
                    hk32_sb[:, hp * 64:(hp + 1) * 64],
                    psB[:, hp * 64:(hp + 1) * 64]),
                    reads=[tB_hp[hp], tb_hk32], writes=[tb_hk32])
                P.op("ve", lambda e, hp=hp: e.tensor_add(
                    hv32_sb[:, hp * 128:(hp + 1) * 128],
                    hv32_sb[:, hp * 128:(hp + 1) * 128],
                    psE[:, hp * 128:(hp + 1) * 128]),
                    reads=[tE_hp[hp], tb_hv32], writes=[tb_hv32])
                for s in hs:
                    P.op("te", lambda e, s=s: e.matmul(
                        s["pool"][:, 0:128],
                        kf[s["sl"], s["toff"]:s["toff"] + 128],
                        qf[s["sl"], s["toff"]:s["toff"] + 128],
                        start=True, stop=True, tile_position=(s["hh"] * 64, 0)),
                        reads=[tb_kf, tb_qf], writes=[s["tp"]])
                for s in hs:
                    P.op("ve", lambda e, s=s: e.tensor_mul(
                        s1t_sb[s["g"] % 4][:], s["pool"][:, 0:128], TRI32),
                        reads=[s["tp"], tb_consts], writes=[tb_s1t[s["g"] % 4]])
                for s in hs:
                    P.op("te", lambda e, s=s, pt=pt: e.matmul(
                        s["pool"][0:1, 192:256], ONECOLBF, pt[:, s["sl"]],
                        start=True, stop=True),
                        reads=[tb_pt, tb_consts], writes=[s["tp"]])
                for s in hs:
                    P.op("ve", lambda e, s=s: e.tensor_add(
                        acc[0:1, 1536 + s["h"] * 64:1536 + (s["h"] + 1) * 64],
                        acc[0:1, 1536 + s["h"] * 64:1536 + (s["h"] + 1) * 64],
                        s["pool"][0:1, 192:256]),
                        reads=[s["tp"], tb_As], writes=[tb_As])
                for s in hs:
                    P.op("te", lambda e, s=s, pt=pt: e.matmul(
                        s["pool"][:, 128:192], TRIBF, pt[:, s["sl"]],
                        start=True, stop=False),
                        reads=[tb_pt, tb_consts], writes=[s["tp"]])
                    P.op("te", lambda e, s=s: e.matmul(
                        s["pool"][:, 128:192], ONEROWBF,
                        A_sbbf[0:1, s["h"] * 64:(s["h"] + 1) * 64],
                        start=False, stop=True),
                        reads=[tb_Abf, tb_consts], writes=[s["tp"]])
                for s in hs:
                    P.op("te", lambda e, s=s, hp=hp: e.matmul(
                        s["pool"][:, 0:64],
                        qf[s["sl"], s["toff"]:s["toff"] + 128],
                        hk_sb[s["sl"], hp * 64:(hp + 1) * 64],
                        start=True, stop=False, tile_position=(s["hh"] * 64, 0)),
                        reads=[tb_qf, tb_hk], writes=[s["tp"]])
                    P.op("te", lambda e, s=s, pt=pt: e.matmul(
                        s["pool"][:, 0:64], s1t_sb[s["g"] % 4][:], pt[:, s["sl"]],
                        start=False, stop=True),
                        reads=[tb_s1t[s["g"] % 4], tb_pt], writes=[s["tp"]])
                for s in hs:
                    P.op("ve", lambda e, s=s: e.reciprocal(
                        rc_sb[s["g"] % 4][:], s["pool"][:, 128:192]),
                        reads=[s["tp"]], writes=[tb_rc[s["g"] % 4]])
                    P.op("ve", lambda e, s=s: e.tensor_mul(
                        ok_sb[s["g"] % 4][:], s["pool"][:, 0:64], rc_sb[s["g"] % 4][:]),
                        reads=[s["tp"], tb_rc[s["g"] % 4]], writes=[tb_ok[s["g"] % 4]])
                    P.op("ve", lambda e, s=s: e.reduce_max(
                        st_sb[s["g"] % 4][:, 0:1], ok_sb[s["g"] % 4][:], axis=AX, negate=True),
                        reads=[tb_ok[s["g"] % 4]], writes=[tb_st[s["g"] % 4]])
                for s in hs:
                    P.op("se", lambda e, s=s: e.activation(
                        e_sb[s["g"] % 4][:], ok_sb[s["g"] % 4][:], AF.Exp,
                        bias=st_sb[s["g"] % 4][:, 0:1],
                        accum_out=st_sb[s["g"] % 4][:, 1:2]),
                        reads=[tb_ok[s["g"] % 4], tb_st[s["g"] % 4]],
                        writes=[tb_e[s["g"] % 4], tb_st[s["g"] % 4]])
                for s in hs:
                    P.op("ve", lambda e, s=s: e.reciprocal(
                        st_sb[s["g"] % 4][:, 2:3], st_sb[s["g"] % 4][:, 1:2]),
                        reads=[tb_st[s["g"] % 4]], writes=[tb_st[s["g"] % 4]])
                    P.op("ve", lambda e, s=s, hp=hp: e.scalar_tensor_tensor(
                        u_pair[hp % 3][:, s["hh"] * 64:(s["hh"] + 1) * 64],
                        e_sb[s["g"] % 4][:], st_sb[s["g"] % 4][:, 2:3], rc_sb[s["g"] % 4][:],
                        op0=ALU.mult, op1=ALU.mult),
                        reads=[tb_e[s["g"] % 4], tb_st[s["g"] % 4], tb_rc[s["g"] % 4]],
                        writes=[tb_up[hp % 3]])
                P.op("te", lambda e, hp=hp, UFW=UFW: e.transpose(
                    UFW, u_pair[hp % 3][:], IDNBF),
                    reads=[tb_up[hp % 3], tb_consts], writes=[tbk])
                P.op("se", lambda e, hp=hp, UFW=UFW: e.activation(uf_sb[hp % 3][:], UFW, AF.Copy),
                     reads=[tbk], writes=[tb_uf[hp % 3]])
                for s in hs:
                    P.op("te", lambda e, s=s, hp=hp: e.matmul(
                        s["pool"][:, 256:384],
                        p32f[s["sl"], s["toff"]:s["toff"] + 128],
                        uf_sb[hp % 3][s["sl"], :],
                        start=True, stop=True, tile_position=(s["hh"] * 64, 0)),
                        reads=[tb_p32f, tb_uf[hp % 3]], writes=[s["tp"]])
                for s in hs:
                    P.op("ve", lambda e, s=s: e.tensor_mul(
                        wov_sb[s["g"] % 4][:], s["pool"][:, 256:384], TRI32),
                        reads=[s["tp"], tb_consts], writes=[tb_wov[s["g"] % 4]])
                for s in hs:
                    P.op("te", lambda e, s=s, hp=hp: e.matmul(
                        s["pool"][:, 384:512], uf_sb[hp % 3][s["sl"], :],
                        hv_sb[s["sl"], hp * 128:(hp + 1) * 128],
                        start=True, stop=False, tile_position=(s["hh"] * 64, 0)),
                        reads=[tb_uf[hp % 3], tb_hv], writes=[s["tp"]])
                    P.op("te", lambda e, s=s, c=c: e.matmul(
                        s["pool"][:, 384:512], wov_sb[s["g"] % 4][:],
                        vt[:, c * 2048 + s["h"] * 128: c * 2048 + (s["h"] + 1) * 128],
                        start=False, stop=True),
                        reads=[tb_wov[s["g"] % 4], tb_vt], writes=[s["tp"]])
                for s in hs:
                    P.op("se", lambda e, s=s: e.activation(
                        sqj2[s["g"] % 4][:], s["pool"][:, 384:512], AF.Square,
                        accum_out=st2_sb[s["g"] % 4][:, 0:1]),
                        reads=[s["tp"]], writes=[tb_st2[s["g"] % 4], tb_sqj2[s["g"] % 4]])
                    P.op("se", lambda e, s=s: e.activation(
                        st2_sb[s["g"] % 4][:, 1:2], st2_sb[s["g"] % 4][:, 0:1], AF.Ln,
                        bias=1e-5, scale=1.0 / 128.0),
                        reads=[tb_st2[s["g"] % 4]], writes=[tb_st2[s["g"] % 4]])
                    P.op("se", lambda e, s=s: e.activation(
                        st2_sb[s["g"] % 4][:, 2:3], st2_sb[s["g"] % 4][:, 1:2],
                        AF.Exp, scale=-0.5),
                        reads=[tb_st2[s["g"] % 4]], writes=[tb_st2[s["g"] % 4]])
                for s in hs:
                    P.op("ve", lambda e, s=s, c=c: e.scalar_tensor_tensor(
                        og_sb[s["g"] % 4][:], s["pool"][:, 384:512], st2_sb[s["g"] % 4][:, 2:3],
                        sg[:, c * 2048 + s["h"] * 128: c * 2048 + (s["h"] + 1) * 128],
                        op0=ALU.mult, op1=ALU.mult),
                        reads=[s["tp"], tb_st2[s["g"] % 4], tb_sg], writes=[tb_og[s["g"] % 4]])
                for s in hs:
                    P.op("te", lambda e, s=s, OTW=OTW: e.transpose(
                        OTW[s["hh"]], og_sb[s["g"] % 4][:], IDNBF),
                        reads=[tb_og[s["g"] % 4], tb_consts], writes=[tbk])
                for s in hs:
                    P.op("ve", lambda e, s=s, c=c, OTW=OTW: e.tensor_copy(
                        of_sb[:, s["h"] * TC + c * 128: s["h"] * TC + (c + 1) * 128],
                        OTW[s["hh"]]),
                        reads=[tbk], writes=[tb_of])

        ap = dumpt("of", [128, 16 * TC], BF16)
        if ap is not None:
            P.op("sp", lambda e, ap=ap: e.dma_start(out=ap, in_=of_sb[:]),
                 reads=[tb_of], dma=ch_dump)

    P.phase = "ph5_wo"
    if phases >= 6:
        free_scope(scopeS2, [tb_p32t[0], tb_p32t[1], tb_kt[0], tb_kt[1], tb_A1,
                             tb_As, tb_Abf, tb_hk, tb_hv, tb_hk32, tb_hv32]
                   + tb_up + tb_uf + tb_s1t + tb_wov + tb_rc + tb_ok + tb_e
                   + tb_st + tb_st2 + tb_og + tb_sqj2)
        free_scope(scopeAcc, [tb_acc])
        free_scope(scopeS1, [tb_p32f, tb_kf, tb_qf, tb_vt, tb_sg])
        ch_wo = P.chan("wo")
        ch_xw = [P.chan(f"xw{i}") for i in range(2)]
        ch_scx = [P.chan(f"scx{i}") for i in range(2)]
        scopeW = ExitStack()
        wo_sb = sb("wo_sb", [128, 32768], BF16, scope=scopeW)
        tb_wo = mkbuf("wo")
        xb2 = [sb(f"xb2_{i}", [128, D], F32, scope=scopeW) for i in range(2)]
        tb_xb2 = [mkbuf(f"xb2_{i}") for i in range(2)]
        x1b = [sb(f"x1b_{i}", [128, D], F32, scope=scopeW) for i in range(2)]
        tb_x1b = [mkbuf(f"x1b_{i}") for i in range(2)]
        yn2 = [sb(f"yn2_{i}", [128, D], BF16, scope=scopeW) for i in range(2)]
        tb_yn2 = [mkbuf(f"yn2_{i}") for i in range(2)]
        ln2st = [sb(f"ln2st{i}", [128, 8], F32, scope=scopeW) for i in range(2)]
        tb_ln2st = [mkbuf(f"ln2st{i}") for i in range(2)]
        sqj3 = [sb(f"sqj3{i}", [128, D], BF16, scope=scopeW) for i in range(2)]
        tb_sqj3 = [mkbuf(f"sqj3{i}") for i in range(2)]

        ch_wo4 = [P.chan(f"wo4_{i}") for i in range(4)]
        tb_wo_sl = [mkbuf(f"wo_sl{i}") for i in range(4)]
        for q, eng in enumerate(("sp", "gp", "se", "gp")):
            P.op(eng, lambda e, q=q: e.dma_start(
                out=wo_sb[:, q * 8192:(q + 1) * 8192],
                in_=woB[:, q * 8192:(q + 1) * 8192]),
                reads=[tb_woB], writes=[tb_wo_sl[q]], dma=ch_wo4[q])
        P.op("se", lambda e: e.activation(wo_sb[:], wo_sb[:], AF.Copy, scale=gnw_sb[:, 0:1]),
             reads=[tb_consts] + tb_wo_sl, writes=[tb_wo])

        for i in range(NCH):
            P.op("gp", lambda e, i=i: e.dma_start(
                out=xb2[i % 2][:], in_=x_in[i * 128:(i + 1) * 128, :]),
                writes=[tb_xb2[i % 2]], dma=ch_xw[i % 2])
            for dh in range(2):
                (h0, h1), tbs = slots[wi % 3]
                wi += 1
                for vk in range(16):
                    for ns, half in enumerate((h0, h1)):
                        P.op("te", lambda e, i=i, dh=dh, vk=vk, ns=ns, half=half:
                             e.matmul(half,
                                      of_sb[:, vk * TC + i * 128: vk * TC + (i + 1) * 128],
                                      wo_sb[:, vk * 2048 + dh * 1024 + ns * 512:
                                            vk * 2048 + dh * 1024 + (ns + 1) * 512],
                                      start=(vk == 0), stop=(vk == 15)),
                             reads=[tb_of, tb_wo], writes=list(tbs))
                for ns, half in enumerate((h0, h1)):
                    off = dh * 1024 + ns * 512
                    P.op("ve", lambda e, i=i, off=off, half=half: e.tensor_add(
                        x1b[i % 2][:, off:off + 512], half, xb2[i % 2][:, off:off + 512]),
                        reads=list(tbs) + [tb_xb2[i % 2]], writes=[tb_x1b[i % 2]])
            P.op("gp", lambda e, i=i: e.dma_start(
                out=scrX1[i * 128:(i + 1) * 128, :], in_=x1b[i % 2][:]),
                reads=[tb_x1b[i % 2]], appends=[tb_scrX1], dma=ch_scx[i % 2])
            emit_ln(x1b[i % 2], tb_x1b[i % 2], yn2[i % 2], tb_yn2[i % 2],
                    ln2st[i % 2], tb_ln2st[i % 2], sqj3[i % 2], tb_sqj3[i % 2])
            for dt in range(16):
                bank, tbk = (psF, tF) if dt % 2 == 0 else (psG, tG)
                w = bank[:, (dt // 2) % 4 * 128:((dt // 2) % 4 + 1) * 128]
                P.op("te", lambda e, dt=dt, i=i, w=w: e.transpose(
                    w, yn2[i % 2][:, dt * 128:(dt + 1) * 128], IDNBF),
                    reads=[tb_yn2[i % 2], tb_consts], writes=[tbk])
                P.op("ve", lambda e, dt=dt, i=i, w=w: e.tensor_copy(
                    ynT[:, dt * TC + i * 128: dt * TC + (i + 1) * 128], w),
                    reads=[tbk], writes=[tb_ynT])
        ap = dumpt("x1", [TC, 2048], F32)
        if ap is not None:
            P.op("sp", lambda e, ap=ap: e.dma_start(out=ap, in_=scrX1),
                 reads=[tb_scrX1], dma=ch_dump)
        ap = dumpt("ynT", [128, 16 * TC], BF16)
        if ap is not None:
            P.op("sp", lambda e, ap=ap: e.dma_start(out=ap, in_=ynT[:]),
                 reads=[tb_ynT], dma=ch_dump)
        free_scope(scopeW, [tb_wo] + tb_ln2st + tb_sqj3 + tb_wo_sl + tb_xb2 + tb_x1b + tb_yn2)
        free_scope(scopeO, [tb_of])

    if phases >= 7:
        P.phase = "ph6_ffnA"
        ch_f1 = [P.chan(f"f1_{i}") for i in range(3)]
        ch_f3 = [P.chan(f"f3_{i}") for i in range(4)]
        ch_s1 = [P.chan(f"s1_{i}") for i in range(2)]
        scopeFA = ExitStack()
        w1b = [sb(f"w1b_{i}", [128, 8192], BF16, scope=scopeFA) for i in range(3)]
        w3b = [sb(f"w3b_{i}", [128, 8192], BF16, scope=scopeFA) for i in range(3)]
        tb_w1b = [mkbuf(f"w1b{i}") for i in range(3)]
        tb_w3b = [mkbuf(f"w3b{i}") for i in range(3)]
        silu32 = [sb(f"silu32_{i}", [128, 1024], F32, scope=scopeFA) for i in range(2)]
        tb_silu = [mkbuf(f"silu{i}") for i in range(2)]
        hidb = [sb(f"hidb_{i}", [128, 1024], BF16, scope=scopeFA) for i in range(2)]
        tb_hidb = [mkbuf(f"hidb{i}") for i in range(2)]
        h1b = [sb(f"h1b_{i}", [128, 1024], BF16, scope=scopeFA) for i in range(2)]
        tb_h1b = [mkbuf(f"h1b{i}") for i in range(2)]
        def emit_h128(a):
            nonlocal wi
            (f0, f1), tbsF = slots[wi % 3]
            wi += 1
            for ns, half in enumerate((f0, f1)):
                P.op("te", lambda e, a=a, ns=ns, half=half: e.matmul(
                    half, H128BF, hidb[a % 2][:, ns * 512:(ns + 1) * 512],
                    start=True, stop=True),
                    reads=[tb_hidb[a % 2], tb_consts], writes=list(tbsF))
            P.op("se", lambda e, a=a, h=f0: e.activation(
                h1b[a % 2][:, 0:512], h, AF.Copy, scale=FWHT_SCALE),
                reads=list(tbsF), writes=[tb_h1b[a % 2]])
            P.op("ve", lambda e, a=a, h=f1: e.tensor_scalar_mul(
                h1b[a % 2][:, 512:1024], h, FWHT_SCALE),
                reads=list(tbsF), writes=[tb_h1b[a % 2]])
            P.op("sp", lambda e, a=a: e.dma_start(
                out=scr1[a * 128:(a + 1) * 128, :], in_=h1b[a % 2][:]),
                reads=[tb_h1b[a % 2]], appends=[tb_scr1], dma=ch_s1[a % 2])

        def load_w13(g):
            P.op("gp", lambda e, g=g: e.dma_start(
                out=w1b[g % 3][:].rearrange("p (a q) -> p a q", q=2048),
                in_=w1S[g * 4:(g + 1) * 4].rearrange("a p q -> p a q")),
                writes=[tb_w1b[g % 3]], dma=ch_f1[g % 3])
            P.op("gp", lambda e, g=g: e.dma_start(
                out=w3b[g % 3][:].rearrange("p (a q) -> p a q", q=2048),
                in_=w3S[g * 4:(g + 1) * 4].rearrange("a p q -> p a q")),
                writes=[tb_w3b[g % 3]], dma=ch_f3[g % 3])

        load_w13(0)
        load_w13(1)
        load_w13(2)
        for a in range(NA):
            if a % 4 == 0:
                g = a // 4
                if 3 <= g + 2 < NA // 4:
                    load_w13(g + 2)
            ai = a % 4
            gg = (a // 4) % 3
            (u10, u11), tbs1 = slots[wi % 3]
            wi += 1
            (u30, u31), tbs3 = slots[wi % 3]
            wi += 1
            for wb, tb_wb, halves, tbsX in ((w1b[gg], tb_w1b[gg], (u10, u11), tbs1),
                                            (w3b[gg], tb_w3b[gg], (u30, u31), tbs3)):
                for dk in range(16):
                    for ns, half in enumerate(halves):
                        P.op("te", lambda e, dk=dk, ai=ai, wb=wb, ns=ns, half=half:
                             e.matmul(half, wb[:, ai * 2048 + dk * 128: ai * 2048 + (dk + 1) * 128],
                                      ynT[:, dk * TC + ns * 512: dk * TC + (ns + 1) * 512],
                                      start=(dk == 0), stop=(dk == 15)),
                             reads=[tb_wb, tb_ynT], writes=list(tbsX))
            for ns in range(2):
                P.op("se", lambda e, a=a, ns=ns, h=(u10, u11)[ns]: e.activation(
                    silu32[a % 2][:, ns * 512:(ns + 1) * 512], h, AF.Silu),
                    reads=list(tbs1), writes=[tb_silu[a % 2]])
                P.op("ve", lambda e, a=a, ns=ns, h=(u30, u31)[ns]: e.tensor_mul(
                    hidb[a % 2][:, ns * 512:(ns + 1) * 512],
                    silu32[a % 2][:, ns * 512:(ns + 1) * 512], h),
                    reads=list(tbs3) + [tb_silu[a % 2]], writes=[tb_hidb[a % 2]])
            if a > 0:
                emit_h128(a - 1)
        emit_h128(NA - 1)
        free_scope(scopeFA, tb_w1b + tb_w3b + tb_silu + tb_hidb + tb_h1b)
        free_scope(scopeY, [tb_ynT])

        P.phase = "ph6_ffnB1"
        ch_b1 = [P.chan(f"b1_{i}") for i in range(3)]
        hfwht = sb("hfwht", [128, NA * TC], BF16, side="right")
        tb_hf = mkbuf("hfwht")
        scopeFB = ExitStack()
        b1in = [sb(f"b1in_{i}", [128, 1024], BF16, scope=scopeFB) for i in range(3)]
        tb_b1in = [mkbuf(f"b1in{i}") for i in range(3)]
        scr1_g = scr1.rearrange("(a x m) t -> x m a t", x=2, m=64)
        for m in range(64):
            for x in range(2):
                q = (2 * m + x) % 3
                P.op("gp", lambda e, m=m, x=x: e.dma_start(
                    out=b1in[m % 3][x * 64:(x + 1) * 64, :],
                    in_=scr1_g[x, m]),
                    reads=[tb_scr1], writes=[tb_b1in[m % 3]], dma=ch_b1[q])
            (h0, h1), tbs = slots[wi % 3]
            wi += 1
            for ns, half in enumerate((h0, h1)):
                P.op("te", lambda e, m=m, ns=ns, half=half: e.matmul(
                    half, h64d_bf[:],
                    b1in[m % 3][:, ns * 512:(ns + 1) * 512],
                    start=True, stop=True),
                    reads=[tb_b1in[m % 3], tb_consts], writes=list(tbs))
                if ns == 0:
                    P.op("se", lambda e, m=m, half=half: e.activation(
                        hfwht[:, m * TC: m * TC + 512], half, AF.Copy),
                        reads=list(tbs), writes=[tb_hf])
                else:
                    P.op("ve", lambda e, m=m, half=half: e.tensor_copy(
                        hfwht[:, m * TC + 512: m * TC + 1024], half),
                        reads=list(tbs), writes=[tb_hf])
        free_scope(scopeFB, tb_b1in)

        P.phase = "ph6_ffnC"
        ch_w2 = [P.chan(f"w2_{i}") for i in range(2)]
        ch_xp = [P.chan(f"xp_{i}") for i in range(3)]
        scopeFC = ExitStack()
        w2b = [sb(f"w2b_{i}", [128, 8192], BF16, scope=scopeFC) for i in range(2)]
        tb_w2bg = [mkbuf(f"w2bg{i}") for i in range(2)]
        tb_w2bs = [mkbuf(f"w2bs{i}") for i in range(2)]
        ch_w2s = [P.chan(f"w2s_{i}") for i in range(2)]
        x1p = [sb(f"x1p_{i}", [128, 1024], F32, scope=scopeFC) for i in range(3)]
        tb_x1p = [mkbuf(f"x1p{i}") for i in range(3)]
        ffnf = [sb(f"ffnf_{i}", [128, 1024], F32, scope=scopeFC) for i in range(2)]
        tb_ffnf = [mkbuf(f"ffnf{i}") for i in range(2)]
        outb = [sb(f"outb_{i}", [128, 1024], F32, scope=scopeFC) for i in range(2)]
        tb_outb = [mkbuf(f"outb{i}") for i in range(2)]
        ch_ob = [P.chan(f"ob{i}") for i in range(2)]
        tb_outd = mkbuf("outd")
        obi = 0
        for dt in range(16):
            P.op("gp", lambda e, dt=dt: e.dma_start(
                out=w2b[dt % 2][:], in_=w2S[dt]),
                writes=[tb_w2bg[dt % 2]], dma=ch_w2[dt % 2])
            P.op("se", lambda e, dt=dt: e.dma_start(
                out=x1p[dt % 3][:].rearrange("p (i j) -> p i j", j=128),
                in_=scrX1.rearrange("(i p) q -> p i q", p=128)[:, :, dt * 128:(dt + 1) * 128]),
                reads=[tb_scrX1], writes=[tb_x1p[dt % 3]], dma=ch_xp[dt % 3])
            (h0, h1), tbs = slots[wi % 3]
            wi += 1
            for a2 in range(NA):
                for ns, half in enumerate((h0, h1)):
                    P.op("te", lambda e, dt=dt, a2=a2, ns=ns, half=half: e.matmul(
                        half, w2b[dt % 2][:, a2 * 128:(a2 + 1) * 128],
                        hfwht[:, a2 * TC + ns * 512: a2 * TC + (ns + 1) * 512],
                        start=(a2 == 0), stop=(a2 == NA - 1)),
                        reads=[tb_w2bg[dt % 2], tb_hf], writes=list(tbs))
            for ns, half in enumerate((h0, h1)):
                P.op("ve", lambda e, dt=dt, ns=ns, half=half: e.tensor_copy(
                    ffnf[dt % 2][:, ns * 512:(ns + 1) * 512], half),
                    reads=list(tbs), writes=[tb_ffnf[dt % 2]])
            ob, tb_ob = outb[dt % 2], tb_outb[dt % 2]
            for i in range(NCH):
                bank, tbk = (psC, tC_) if i % 2 == 0 else (psD, tD)
                w = bank[:, (i // 2) % 2 * 128:((i // 2) % 2 + 1) * 128]
                P.op("te", lambda e, dt=dt, i=i, w=w: e.transpose(
                    w, ffnf[dt % 2][:, i * 128:(i + 1) * 128], IDN32),
                    reads=[tb_ffnf[dt % 2], tb_consts], writes=[tbk])
                P.op("ve", lambda e, dt=dt, i=i, w=w, ob=ob: e.tensor_add(
                    ob[:, i * 128:(i + 1) * 128], w, x1p[dt % 3][:, i * 128:(i + 1) * 128]),
                    reads=[tbk, tb_x1p[dt % 3]], writes=[tb_ob])
            P.op("sp", lambda e, dt=dt, ob=ob: e.dma_start(
                out=out_dram.rearrange("(i p) q -> p i q", p=128)[:, :, dt * 128:(dt + 1) * 128],
                in_=ob[:].rearrange("p (i j) -> p i j", j=128)),
                reads=[tb_ob], appends=[tb_outd], dma=ch_ob[dt % 2])
        scopeFC.close()

    if phases < 7:
        z128 = sb("z128", [128, 128], F32)
        tb_z = mkbuf("z")
        P.op("ve", lambda e: e.memset(z128[:], 0.0), reads=[tb_consts], writes=[tb_z])
        for i in range(NCH):
            for j in range(16):
                P.op("sp", lambda e, i=i, j=j: e.dma_start(
                    out=out_dram[i * 128:(i + 1) * 128, j * 128:(j + 1) * 128],
                    in_=z128[:]),
                    reads=[tb_z], dma=ch_out)

    block = E_(nc.Block())
    P.emit(nc, block, sems_c, sem_cc, mksem)
    ctx.close()
    return nc



def _relayout(inputs):
    f = np.ascontiguousarray
    Wq = inputs["L0_attn_Wq"]; Wk = inputs["L0_attn_Wk"]; Ws = inputs["L0_attn_Ws"]
    Wv = inputs["L0_attn_Wv"]; Wg = inputs["L0_attn_Wg"]; Wo = inputs["L0_attn_Wo"]
    W1 = inputs["L0_ffn_W1"]; W2 = inputs["L0_ffn_W2"]; W3 = inputs["L0_ffn_W3"]

    def colblocks(W):
        A = np.asarray(W, np.float32).reshape(16, 128, 8, 128)
        return f(A.transpose(2, 1, 0, 3).reshape(8, 128, 2048))

    def vhalves(W):
        A = np.asarray(W, np.float32).reshape(16, 128, 2, 1024)
        return f(A.transpose(1, 2, 0, 3).reshape(128, 2, 16384))

    def wo_lay(W):
        A = np.asarray(W, np.float32).reshape(16, 128, 2048)
        return f(A.transpose(1, 0, 2).reshape(128, 32768))

    def w13_lay(W):
        A = np.asarray(W, np.float32).reshape(64, 128, 16, 128)
        return f(A.transpose(0, 3, 2, 1).reshape(64, 128, 2048))

    def w2_lay(W):
        A = np.asarray(W, np.float32).reshape(16, 128, 64, 2, 64)
        return f(A.transpose(0, 3, 2, 4, 1).reshape(16, 128, 8192))

    def hmat(n):
        Hm = np.ones((1, 1), np.float32)
        while Hm.shape[0] < n:
            Hm = np.block([[Hm, Hm], [Hm, -Hm]]).astype(np.float32)
        return Hm

    com = {
        "wqS": colblocks(Wq), "wkS": colblocks(Wk), "wsS": colblocks(Ws),
        "wvS": vhalves(Wv), "wgS": vhalves(Wg), "woS": wo_lay(Wo),
        "w1S": w13_lay(W1), "w3S": w13_lay(W3), "w2S": w2_lay(W2),
        "gnw": f(np.asarray(inputs["L0_attn_g_norm_weight"], np.float32).reshape(128, 1)),
        "tri": np.triu(np.ones((128, 128), np.float32)),
        "idn": np.eye(128, dtype=np.float32),
        "h64d": np.kron(np.eye(2, dtype=np.float32), hmat(64)), "h128c": hmat(128),
        "onesc": np.ones((128, 128), np.float32),
    }
    xfull = np.asarray(inputs["L0_attn_hidden_states"], np.float32).reshape(T, D)
    maps = []
    for i in range(NCORE):
        m = dict(com)
        m["x"] = f(xfull[i * TC:(i + 1) * TC])
        pm = np.zeros((128, 8), np.float32)
        pm[:, :i] = 1.0
        m["pmask"] = pm
        maps.append(m)
    return maps


_CACHE = {}


def kernel(**inputs):
    if "main" not in _CACHE:
        _CACHE["main"] = build_program()
    nc = _CACHE["main"]
    in_maps = _relayout(inputs)
    res = run_bass_kernel_spmd(nc, in_maps, core_ids=list(range(NCORE)))
    out = np.concatenate([r["out"] for r in res.results], axis=0)
    return out.reshape(1, T, D)
